# version 22
# baseline (speedup 1.0000x reference)
import numpy as np
from contextlib import ExitStack
import concourse.bass as bass
import concourse.mybir as mybir
from concourse.bass_utils import run_bass_kernel_spmd
from concourse.aot_env import allow_ici_all_to_all

F32 = mybir.dt.float32
BF16 = mybir.dt.bfloat16
I32 = mybir.dt.int32
AF = mybir.ActivationFunctionType
ALU = mybir.AluOpType

NCORES = 8
B, S, DM = 2, 4096, 4096
T = B * S
TPC = T // NCORES
D = 128
QSL = 512
E = 64
SCALE = float((1.0 / np.sqrt(np.float32(128.0))) / 4.0)
EPS = 1e-5
FMIN = -3.0e38

ENGS = ("tensor", "vector", "scalar", "gpsimd", "sync")
TE, V, SC, G, SY = ENGS


class Sem:
    def __init__(self, handle):
        self.h = handle
        self.n = 0


class Rec:
    def __init__(self):
        self.ops = {e: [] for e in ENGS}

    def emit(self, eng, fn, inc=None):
        incs = []
        if inc:
            for s, by in inc:
                s.n += by
                incs.append((s, by))
        self.ops[eng].append(("op", fn, incs))

    def wait(self, eng, sem, val=None):
        v = sem.n if val is None else val
        if v <= 0:
            return
        lst = self.ops[eng]
        if lst and lst[-1][0] == "wait" and lst[-1][1] is sem:
            if v > lst[-1][2]:
                lst[-1] = ("wait", sem, v)
            return
        lst.append(("wait", sem, v))

    def play(self, eng_name, e):
        for kind, a, b in self.ops[eng_name]:
            if kind == "wait":
                e.wait_ge(a.h, b)
            else:
                ins = a(e)
                for s, by in b:
                    ins.then_inc(s.h, by)


def build_nc():
    nc = bass.Bass("TRN2", target_bir_lowering=False, debug=False)

    x_own = nc.dram_tensor("x_own", [TPC, DM], F32, kind="ExternalInput").ap()
    xt = nc.dram_tensor("xt", [DM, T], F32, kind="ExternalInput").ap()
    qc = nc.dram_tensor("qc", [DM, QSL], I32, kind="ExternalInput").ap()
    kc = nc.dram_tensor("kc", [DM, D], I32, kind="ExternalInput").ap()
    vc = nc.dram_tensor("vc", [DM, D], I32, kind="ExternalInput").ap()
    oc = nc.dram_tensor("oc", [DM, DM], I32, kind="ExternalInput").ap()
    qs = nc.dram_tensor("qs", [DM, QSL], F32, kind="ExternalInput").ap()
    ks = nc.dram_tensor("ks", [DM, D], F32, kind="ExternalInput").ap()
    vs = nc.dram_tensor("vs", [DM, D], F32, kind="ExternalInput").ap()
    osr = nc.dram_tensor("osr", [DM, DM], F32, kind="ExternalInput").ap()
    rw = nc.dram_tensor("rw", [E, DM], F32, kind="ExternalInput").ap()
    rb = nc.dram_tensor("rb", [1, E], F32, kind="ExternalInput").ap()
    ebi = nc.dram_tensor("ebi", [E, DM], F32, kind="ExternalInput").ap()
    cosc = nc.dram_tensor("cosc", [128, S], F32, kind="ExternalInput").ap()
    sinc = nc.dram_tensor("sinc", [128, S], F32, kind="ExternalInput").ap()
    eyec = nc.dram_tensor("eyec", [128, 128], F32, kind="ExternalInput").ap()
    tric = nc.dram_tensor("tric", [128, 128], F32, kind="ExternalInput").ap()
    onec = nc.dram_tensor("onec", [1, 128], F32, kind="ExternalInput").ap()
    out = nc.dram_tensor("out", [TPC, DM], F32, kind="ExternalOutput").ap()

    a2ix = nc.dram_tensor("a2ix", [NCORES, QSL, TPC // 2], BF16)
    a2iy = nc.dram_tensor("a2iy", [NCORES, QSL, TPC // 2], BF16)
    a2ox = nc.dram_tensor("a2ox", [NCORES, QSL, TPC // 2], BF16)
    a2oy = nc.dram_tensor("a2oy", [NCORES, QSL, TPC // 2], BF16)

    ctx = ExitStack()
    sb = lambda name, shape, dt: ctx.enter_context(nc.sbuf_tensor(name, shape, dt))

    ABF = sb("abf", [128, 77824], BF16)
    AFR = sb("afr", [128, 8192], F32)
    eye_bf = sb("eye_bf", [128, 128], BF16)
    one_bf = sb("one_bf", [1, 128], BF16)
    rb_bf = sb("rb_bf", [1, E], BF16)
    tri_f = sb("tri_f", [128, 128], F32)
    wrt_sb = sb("wrt_sb", [128, 32 * E], BF16)
    eb_sb = sb("eb_sb", [E, DM], BF16)
    rw_bf = sb("rw_bf", [E, DM], BF16)
    stat = sb("stat", [128, 104], F32)
    lg_sb = sb("lg_sb", [128, E], F32)
    ex_sb = sb("ex_sb", [128, E], F32)
    cur_sb = sb("cur_sb", [128, 2 * E], F32)
    msk_sb = sb("msk_sb", [128, E], F32)

    def A(role, a=0, b=None):
        off, w = role
        if b is None:
            b = w
        assert 0 <= a <= b <= w, (role, a, b)
        return ABF[:, off + a: off + b]

    def FA(a, b):
        return AFR[:, a:b]

    R_QROPE = (0, 4 * T)
    R_KROPE = (32768, T)
    R_VSB = (40960, T)
    R_WQ = (49152, 32 * 512)
    R_WK = (65536, 32 * 128)
    R_WV = (69632, 32 * 128)
    R_DQS = (73728, 4096)
    R_XTS = (73728, 2 * 2048)
    R_XTOWN = (0, 32 * 1024)
    R_XBF = (32768, 2 * 4096)
    R_KVST = (40960, 8192)
    R_COS = (49152, 1024)
    R_SIN = (50176, 1024)
    R_RT1 = (51200, 1024)
    R_RT2 = (52224, 1024)
    PRB = [(49152 + h * 4096, 4096) for h in range(4)]
    R_PTSEG = (65536, 4096)
    ASTG = [(69632 + h * 1024, 1024) for h in range(4)]
    DIAG = [(73728 + h * 128, 128) for h in range(4)]
    R_JUNK = (74240, 2048)
    R_ATM = (0, 32 * 1024)
    R_WO = (32768, 32 * 256)
    R_HPRE = (40960, 8 * 4096)
    R_OQS = (73728, 4096)
    R_H = (0, 8 * 4096)
    R_HT = (74752, 256)
    R_WDS = (75008, 64)
    R_WDT = (75072, 128)

    SRA, SRB = 0, 4096
    XRS = 0
    GUN = 0
    OUTB = 4096

    MXS = [[h * 4 + s for s in range(4)] for h in range(4)]
    M01 = [16 + h for h in range(4)]
    M23 = [20 + h for h in range(4)]
    MXC = [24 + h for h in range(4)]
    NMX = [28 + h for h in range(4)]
    SMS = [32 + h for h in range(4)]
    RCP = [36 + h for h in range(4)]
    C_SUM, C_SSQ, C_MEAN, C_EVAR, C_MSQ, C_VPE, C_RSTD, C_NB = 0, 1, 2, 3, 4, 5, 6, 7
    C_NT1, C_NT2, C_NT3 = 8, 9, 10
    C_M64, C_NM64, C_SUM0, C_WS, C_MJ, C_COEF = 11, 12, 13, 14, 15, 16
    C_P1, C_P2 = 96, 100
    C_SP1, C_SP2, C_MJC = 17, 18, 19
    B_S1, B_S2, B_MEAN, B_NMEAN, B_EVAR, B_MSQ, B_VPE, B_RSTD, B_NB = (
        40, 48, 56, 64, 72, 80, 88, 20, 28)
    C_BN = 32

    def st(c, w=1):
        return stat[:, c:c + w]

    PAp = ctx.enter_context(nc.psum_tensor("pa", [128, 1024], F32))
    PBp = ctx.enter_context(nc.psum_tensor("pb", [128, 1024], F32))
    PDp = ctx.enter_context(nc.psum_tensor("pd", [128, 512], F32))
    PEp = ctx.enter_context(nc.psum_tensor("pe", [128, 512], F32))
    PFp = ctx.enter_context(nc.psum_tensor("pf", [128, 512], F32))
    PGp = ctx.enter_context(nc.psum_tensor("pg", [128, 512], F32))

    sems = {k: Sem(ctx.enter_context(nc.semaphore(k)))
            for k in ("gd", "yd", "ts", "vs", "ss", "cs")}
    gd, yd, tse, vse, sse, cse = (sems[k] for k in ("gd", "yd", "ts", "vs", "ss", "cs"))

    r = Rec()

    def spacer(n=1):
        for _ in range(n):
            r.emit(V, lambda e: e.tensor_scalar(
                out=st(C_SP1), in0=st(C_SP2), scalar1=1.0, scalar2=None,
                op0=ALU.mult))

    r.emit(SC, lambda e: e.preload_act_table(AF.Exp))
    r.emit(G, lambda e: e.dma_start(out=eye_bf[:], in_=eyec), inc=[(gd, 16)])
    r.emit(G, lambda e: e.dma_start(out=one_bf[:], in_=onec), inc=[(gd, 16)])
    r.emit(G, lambda e: e.dma_start(out=rb_bf[:], in_=rb), inc=[(gd, 16)])
    r.emit(G, lambda e: e.dma_start(out=tri_f[:], in_=tric), inc=[(gd, 16)])
    r.emit(G, lambda e: e.dma_start(out=eb_sb[:], in_=ebi), inc=[(gd, 16)])
    r.emit(G, lambda e: e.dma_start(out=rw_bf[:], in_=rw), inc=[(gd, 16)])
    gd_consts = gd.n

    r.wait(TE, gd, gd_consts)

    vse_wq = {}
    for gq in range(8):
        if gq >= 1:
            r.wait(G, vse, vse_wq[gq - 1])
        r.emit(G, lambda e, gq=gq: e.dma_start(
            out=A(R_DQS, 0, 2048).rearrange("p (g c) -> p g c", g=4),
            in_=qc[gq * 512:(gq + 1) * 512, :].rearrange("(g p) c -> p g c", p=128)),
            inc=[(gd, 16)])
        r.emit(G, lambda e, gq=gq: e.dma_start(
            out=A(R_DQS, 2048, 4096).rearrange("p (g c) -> p g c", g=4),
            in_=qs[gq * 512:(gq + 1) * 512, :].rearrange("(g p) c -> p g c", p=128)),
            inc=[(gd, 16)])
        r.wait(V, gd, gd.n)
        r.emit(V, lambda e, gq=gq: e.scalar_tensor_tensor(
            out=A(R_WQ, gq * 2048, (gq + 1) * 2048), in0=A(R_DQS, 0, 2048),
            scalar=-8.0, in1=A(R_DQS, 2048, 4096), op0=ALU.add, op1=ALU.mult),
            inc=[(vse, 1)])
        vse_wq[gq] = vse.n
    r.emit(G, lambda e: e.dma_start(
        out=A(R_KVST, 0, 4096).rearrange("p (g c) -> p g c", g=32), in_=kc.rearrange("(g p) c -> p g c", p=128)),
        inc=[(gd, 16)])
    r.emit(G, lambda e: e.dma_start(
        out=A(R_KVST, 4096, 8192).rearrange("p (g c) -> p g c", g=32), in_=ks.rearrange("(g p) c -> p g c", p=128)),
        inc=[(gd, 16)])
    r.wait(V, gd, gd.n)
    r.emit(V, lambda e: e.scalar_tensor_tensor(
        out=A(R_WK), in0=A(R_KVST, 0, 4096), scalar=-8.0,
        in1=A(R_KVST, 4096, 8192), op0=ALU.add, op1=ALU.mult), inc=[(vse, 1)])
    v_wk = vse.n
    r.wait(G, vse, v_wk)
    r.emit(G, lambda e: e.dma_start(
        out=A(R_KVST, 0, 4096).rearrange("p (g c) -> p g c", g=32), in_=vc.rearrange("(g p) c -> p g c", p=128)),
        inc=[(gd, 16)])
    r.emit(G, lambda e: e.dma_start(
        out=A(R_KVST, 4096, 8192).rearrange("p (g c) -> p g c", g=32), in_=vs.rearrange("(g p) c -> p g c", p=128)),
        inc=[(gd, 16)])
    r.wait(V, gd, gd.n)
    r.emit(V, lambda e: e.scalar_tensor_tensor(
        out=A(R_WV), in0=A(R_KVST, 0, 4096), scalar=-8.0,
        in1=A(R_KVST, 4096, 8192), op0=ALU.add, op1=ALU.mult), inc=[(vse, 1)])
    vse_wall = vse.n

    sse_wrt = {}
    for hc in range(32):
        pp = PDp if hc % 2 == 0 else PEp
        if hc >= 2:
            r.wait(TE, sse, sse_wrt[hc - 2])
        r.emit(TE, lambda e, hc=hc, pp=pp: e.transpose(
            pp[:, 0:32].bitcast(BF16),
            rw_bf[:, hc * 128:(hc + 1) * 128], eye_bf[0:E, 0:E]), inc=[(tse, 1)])
        r.wait(SC, tse, tse.n)
        r.emit(SC, lambda e, hc=hc, pp=pp: e.copy(
            wrt_sb[:, hc * E:(hc + 1) * E],
            pp[:, 0:32].bitcast(BF16)),
            inc=[(sse, 1)])
        sse_wrt[hc] = sse.n

    def wsl(dt, hc):
        if dt < 4:
            return A(R_WQ, hc * 512 + dt * 128, hc * 512 + dt * 128 + 128)
        if dt == 4:
            return A(R_WK, hc * 128, hc * 128 + 128)
        return A(R_WV, hc * 128, hc * 128 + 128)

    def rawsl2(dt, tc, a, b):
        if dt < 4:
            return A(R_QROPE, dt * T + tc * 1024 + a, dt * T + tc * 1024 + b)
        if dt == 4:
            return A(R_KROPE, tc * 1024 + a, tc * 1024 + b)
        return A(R_VSB, tc * 1024 + a, tc * 1024 + b)

    def rawsl(dt, tc):
        if dt < 4:
            return A(R_QROPE, dt * T + tc * 1024, dt * T + (tc + 1) * 1024)
        if dt == 4:
            return A(R_KROPE, tc * 1024, (tc + 1) * 1024)
        return A(R_VSB, tc * 1024, (tc + 1) * 1024)

    sse_proj = {}
    tse_xuse = {}

    def ptgt(dt):
        if dt == 0: return PAp[:, 0:512]
        if dt == 1: return PAp[:, 512:1024]
        if dt == 2: return PBp[:, 0:512]
        if dt == 3: return PBp[:, 512:1024]
        if dt == 4: return PDp[:]
        return PEp[:]

    F_COS = [0, 1024]
    F_SINN = [2048, 3072]
    F_VBNC = 4096
    F_SWP3 = [4608, 5120, 5632]
    F_RT1 = 6144
    F_RT2 = 7168
    gd_cs = {}
    vse_rope = {}
    sse_vt = {}
    vt_tse = {}

    def srcp(tc, j, a, b, p0=0, p1=128):
        if j < 4:
            base = R_QROPE[0] + j * T + tc * 1024
        else:
            base = R_KROPE[0] + tc * 1024
        return ABF[p0:p1, base + a:base + b]

    def emit_vt(gv):
        if gv >= 1:
            r.wait(V, tse, vt_tse[gv - 1])
        r.wait(V, sse, sse_proj[gv * 2 + 1])
        r.emit(V, lambda e, gv=gv: e.tensor_copy(
            AFR[:, F_VBNC:F_VBNC + 512].bitcast(BF16),
            A(R_VSB, gv * 1024, (gv + 1) * 1024)), inc=[(vse, 1)])
        r.wait(TE, vse, vse.n)
        if gv >= 1:
            r.wait(TE, sse, sse_vt[gv - 1])
        for idx in range(8):
            pp = PFp if idx < 4 else PGp
            r.emit(TE, lambda e, idx=idx, pp=pp: e.transpose(
                pp[:, (idx % 4) * 64:(idx % 4) * 64 + 64].bitcast(BF16),
                AFR[:, F_VBNC + idx * 64:F_VBNC + (idx + 1) * 64].bitcast(BF16),
                eye_bf[:]),
                inc=[(tse, 1)] if idx in (3, 7) else None)
        vt_tse[gv] = tse.n
        r.wait(SC, tse, tse.n)
        r.emit(SC, lambda e, gv=gv: e.copy(
            A(R_VSB, gv * 1024, gv * 1024 + 512), PFp[:, 0:256].bitcast(BF16)))
        r.emit(SC, lambda e, gv=gv: e.copy(
            A(R_VSB, gv * 1024 + 512, gv * 1024 + 1024), PGp[:, 0:256].bitcast(BF16)),
            inc=[(sse, 1)])
        sse_vt[gv] = sse.n

    def emit_rope(tc):
        parc = tc % 2
        pos0 = (tc * 1024) % S
        if tc >= 2:
            r.wait(G, vse, vse_rope[(tc - 2) * 5 + 4])
        r.emit(G, lambda e, pos0=pos0, parc=parc: e.dma_start(
            out=FA(F_COS[parc], F_COS[parc] + 1024),
            in_=cosc[:, pos0:pos0 + 1024]), inc=[(gd, 16)])
        r.emit(G, lambda e, pos0=pos0, parc=parc: e.dma_start(
            out=FA(F_SINN[parc], F_SINN[parc] + 1024),
            in_=sinc[:, pos0:pos0 + 1024]), inc=[(gd, 16)])
        gd_cs[tc] = gd.n
        r.wait(SY, sse, sse_proj[tc * 2 + 1])
        r.wait(V, sse, sse_proj[tc * 2 + 1])
        r.wait(V, gd, gd_cs[tc])
        for j in range(5):
            idx = tc * 5 + j
            sl = F_SWP3[idx % 3]
            if idx >= 3:
                r.wait(SY, vse, vse_rope[idx - 3])
            r.emit(SY, lambda e, tc=tc, j=j, sl=sl: e.dma_start(
                out=AFR[0:64, sl:sl + 512].bitcast(BF16),
                in_=srcp(tc, j, 0, 1024, 64, 128)), inc=[(yd, 16)])
            r.emit(SY, lambda e, tc=tc, j=j, sl=sl: e.dma_start(
                out=AFR[64:128, sl:sl + 512].bitcast(BF16),
                in_=srcp(tc, j, 0, 1024, 0, 64)), inc=[(yd, 16)])
            r.wait(V, yd, yd.n)
            r.emit(V, lambda e, tc=tc, j=j, parc=parc: e.tensor_tensor(
                out=FA(F_RT1, F_RT1 + 1024), in0=srcp(tc, j, 0, 1024),
                in1=FA(F_COS[parc], F_COS[parc] + 1024), op=ALU.mult))
            r.emit(V, lambda e, parc=parc, sl=sl: e.tensor_tensor(
                out=FA(F_RT2, F_RT2 + 1024),
                in0=AFR[:, sl:sl + 512].bitcast(BF16),
                in1=FA(F_SINN[parc], F_SINN[parc] + 1024), op=ALU.mult))
            r.emit(V, lambda e, tc=tc, j=j: e.tensor_tensor(
                out=srcp(tc, j, 0, 1024), in0=FA(F_RT1, F_RT1 + 1024),
                in1=FA(F_RT2, F_RT2 + 1024), op=ALU.add), inc=[(vse, 1)])
            vse_rope[idx] = vse.n

    r.wait(G, vse, vse_wall)
    r.wait(TE, vse, vse_wall)
    r.wait(TE, sse, sse.n)
    for tc in range(8):
        for th in range(2):
            if th == 1 and tc >= 1:
                emit_vt(tc - 1)
            for hc2 in range(16):
                key = (tc * 2 + th) * 16 + hc2
                buf = key % 4
                if key >= 4:
                    r.wait(G, tse, tse_xuse[key - 4])
                r.emit(G, lambda e, tc=tc, th=th, hc2=hc2, buf=buf: e.dma_start(
                    out=A(R_XTS, buf * 1024, buf * 1024 + 1024).rearrange(
                        "p (g c) -> p g c", g=2),
                    in_=xt[hc2 * 256:(hc2 + 1) * 256,
                           tc * 1024 + th * 512:tc * 1024 + th * 512 + 512]
                    .rearrange("(g p) c -> p g c", p=128)), inc=[(gd, 16)])
                r.wait(TE, gd, gd.n)
                if hc2 == 0 and (tc * 2 + th) >= 1:
                    r.wait(TE, sse, sse_proj[tc * 2 + th - 1])
                for g in range(2):
                    hc = hc2 * 2 + g
                    for dt in range(6):
                        last = (dt == 5 and g == 1)
                        r.emit(TE, lambda e, dt=dt, hc=hc, buf=buf, g=g: e.matmul(
                            ptgt(dt),
                            wsl(dt, hc),
                            A(R_XTS, buf * 1024 + g * 512,
                              buf * 1024 + (g + 1) * 512),
                            start=(hc == 0), stop=(hc == 31)),
                            inc=[(tse, 1)] if last else None)
                tse_xuse[key] = tse.n
                if th == 0 and hc2 == 0 and tc >= 1:
                    emit_rope(tc - 1)
            r.wait(SC, tse, tse.n)
            for dt in range(6):
                r.emit(SC, lambda e, dt=dt, tc=tc, th=th: e.copy(
                    rawsl2(dt, tc, th * 512, th * 512 + 512), ptgt(dt)),
                    inc=[(sse, 1)] if dt == 5 else None)
            sse_proj[tc * 2 + th] = sse.n
    emit_vt(7)
    emit_rope(7)
    sse_projall = sse.n
    vse_ropeall = vse.n
    sse_vtall = sse.n

    pairs = [(b, t) for b in range(2) for t in range(32)]
    SCR = [PAp, PBp]
    scr_free = {}
    seg_ctr = [0]
    ptp = [PEp, PFp]
    ptp_free = {}
    pt_last = [0]
    sr_busy = {}
    yd_stage = [0]
    yd_stx = [0]
    cse_ax = [0]
    pd_free = [0]

    def lk(t):
        return (t + 1) * 128

    r.wait(TE, vse, vse_ropeall)
    r.wait(TE, sse, sse_vtall)

    def emit_head01(b, t):
        emit_head_full(b, t, 0)
        emit_head_full(b, t, 1)
        emit_cmb(t, 0)

    def emit_head_full(b, t, h):
        segs = (lk(t) + 1023) // 1024
        for s in range(segs):
            emit_seg_mm(b, t, h, s)
            emit_seg_evac(b, t, h, s)

    seg_reg = {}

    def emit_seg_mm(b, t, h, s):
        if True:
            a = s * 1024
            bnd = min(lk(t), a + 1024)
            w = bnd - a
            reg = seg_ctr[0] % 2
            seg_ctr[0] += 1
            seg_reg[(h, s)] = reg
            r.wait(TE, vse, scr_free.get(reg, 0))
            for half in range((w + 511) // 512):
                wa = half * 512
                wb = min(w, wa + 512)
                r.emit(TE, lambda e, b=b, t=t, h=h, a=a, wa=wa, wb=wb, reg=reg: e.matmul(
                    SCR[reg][:, wa:wb],
                    A(R_QROPE, h * T + b * S + t * 128, h * T + b * S + t * 128 + 128),
                    A(R_KROPE, b * S + a + wa, b * S + a + wb),
                    start=True, stop=True),
                    inc=[(tse, 1)] if wb == w else None)
            seg_tse[(h, s)] = tse.n

    def emit_seg_evac(b, t, h, s):
        srh = h % 2
        sro = SRA if srh == 0 else SRB
        if True:
            a = s * 1024
            bnd = min(lk(t), a + 1024)
            w = bnd - a
            reg = seg_reg[(h, s)]
            r.wait(V, tse, seg_tse[(h, s)])
            if s == 0:
                r.wait(V, sse, sr_busy.get(srh, 0))
            if a <= t * 128 < bnd:
                dloc = t * 128 - a
                r.emit(V, lambda e, reg=reg, dloc=dloc: e.tensor_tensor(
                    out=SCR[reg][:, dloc:dloc + 128], in0=SCR[reg][:, dloc:dloc + 128],
                    in1=tri_f[:], op=ALU.add))
            r.emit(V, lambda e, reg=reg, w=w, sro=sro, a=a, h=h, s=s: e.tensor_scalar(
                out=FA(sro + a, sro + a + w), in0=SCR[reg][:, 0:w],
                scalar1=SCALE, scalar2=FMIN,
                op0=ALU.mult, op1=ALU.max, accum_out=st(MXS[h][s])),
                inc=[(vse, 1)])
            scr_free[reg] = vse.n

    def emit_cmb(t, h):
        ns = (lk(t) + 1023) // 1024
        if ns == 2:
            r.emit(V, lambda e, h=h: e.tensor_tensor(
                out=st(MXC[h]), in0=st(MXS[h][0]), in1=st(MXS[h][1]), op=ALU.max))
        elif ns == 3:
            r.emit(V, lambda e, h=h: e.tensor_tensor(
                out=st(M01[h]), in0=st(MXS[h][0]), in1=st(MXS[h][1]), op=ALU.max))
            spacer()
            r.emit(V, lambda e, h=h: e.tensor_tensor(
                out=st(MXC[h]), in0=st(M01[h]), in1=st(MXS[h][2]), op=ALU.max))
        elif ns == 4:
            r.emit(V, lambda e, h=h: e.tensor_tensor(
                out=st(M01[h]), in0=st(MXS[h][0]), in1=st(MXS[h][1]), op=ALU.max))
            r.emit(V, lambda e, h=h: e.tensor_tensor(
                out=st(M23[h]), in0=st(MXS[h][2]), in1=st(MXS[h][3]), op=ALU.max))
            spacer()
            r.emit(V, lambda e, h=h: e.tensor_tensor(
                out=st(MXC[h]), in0=st(M01[h]), in1=st(M23[h]), op=ALU.max))

    def mxcol(t, h):
        ns = (lk(t) + 1023) // 1024
        return MXS[h][0] if ns == 1 else MXC[h]

    def emit_nmx(t, h):
        r.emit(V, lambda e, h=h, t=t: e.tensor_scalar(
            out=st(NMX[h]), in0=st(mxcol(t, h)), scalar1=-1.0, scalar2=None,
            op0=ALU.mult), inc=[(vse, 1)])

    def emit_exp(b, t, h):
        sro = SRA if h % 2 == 0 else SRB
        r.wait(SC, vse, vse.n)
        r.wait(SC, tse, pt_last[0])
        r.emit(SC, lambda e, h=h, t=t, sro=sro: e.activation(
            A(PRB[h], 0, lk(t)), FA(sro, sro + lk(t)), AF.Exp,
            bias=st(NMX[h]), scale=1.0), inc=[(sse, 1)])
        sr_busy[h % 2] = sse.n
        exp_sse[h] = sse.n

    sms_junk = {}
    sms_direct = {}
    exp_sse = {}
    SMB = [40 + h for h in range(4)]

    def emit_cast(b, t, h):
        sro = SRA if h % 2 == 0 else SRB
        L = lk(t)
        L2 = (L // 256) * 128
        if L2 == 0:
            L2 = L
        r.emit(SC, lambda e, h=h, L2=L2: e.activation(
            A(R_JUNK, 0, L2),
            A(PRB[h], 0, L2), AF.Copy, bias=0.0, scale=1.0,
            accum_out=st(SMS[h])), inc=[(sse, 1)])
        sms_junk[h] = sse.n
        sms_direct[h] = (L2 == L)
        if L2 < L:
            r.wait(V, sse, exp_sse[h])
            r.emit(V, lambda e, h=h, L2=L2, L=L: e.tensor_scalar(
                out=A(PRB[h], L2, L), in0=A(PRB[h], L2, L),
                scalar1=1.0, scalar2=0.0, op0=ALU.mult, op1=ALU.add,
                accum_out=st(SMB[h])), inc=[(vse, 1)])

    def emit_add(h):
        if sms_direct[h]:
            return
        r.wait(V, sse, sms_junk[h])
        r.emit(V, lambda e, h=h: e.tensor_tensor(
            out=st(SMS[h]), in0=st(SMS[h]), in1=st(SMB[h]), op=ALU.add))

    def emit_recip(h):
        if sms_direct[h]:
            r.wait(V, sse, sms_junk[h])
        r.emit(V, lambda e, h=h: e.reciprocal(st(RCP[h]), st(SMS[h])))

    def emit_diag(h, last=False):
        r.wait(V, tse, pt_last[0])
        r.emit(V, lambda e, h=h: e.tensor_scalar(
            out=A(DIAG[h]), in0=eye_bf[:], scalar1=st(RCP[h]), scalar2=None,
            op0=ALU.mult), inc=[(vse, 1)] if last else None)

    def emit_attn_part2(b, t, nxt):
        emit_nmx(t, 0)
        emit_exp(b, t, 0)
        emit_cmb(t, 1)
        emit_head_full(b, t, 2)
        emit_nmx(t, 1)
        emit_exp(b, t, 1)
        emit_cmb(t, 2)
        emit_head_full(b, t, 3)
        emit_nmx(t, 2)
        emit_exp(b, t, 2)
        emit_cmb(t, 3)
        spacer()
        emit_nmx(t, 3)
        emit_exp(b, t, 3)
        emit_cast(b, t, 0)
        emit_cast(b, t, 1)
        if nxt is not None:
            emit_head_full(nxt[0], nxt[1], 0)
        emit_cast(b, t, 2)
        emit_cast(b, t, 3)
        if nxt is not None:
            emit_head_full(nxt[0], nxt[1], 1)
        for h in range(4):
            emit_add(h)
        for h in range(4):
            emit_recip(h)
        for h in range(4):
            emit_diag(h, last=(h == 3))
        if nxt is not None:
            emit_cmb(nxt[1], 0)
        return vse.n

    seg_tse = {}

    def emit_pv(b, t, v_diag):
        nkt = t + 1
        r.wait(TE, vse, v_diag)
        if isinstance(pd_free[0], tuple):
            r.wait(TE, pd_free[0][0], pd_free[0][1])
        pt_ready = {}
        for kt in range(nkt + 1):
            if kt < nkt:
                par = kt % 2
                sl = kt % 8
                if par in ptp_free:
                    semx, vx = ptp_free[par]
                    r.wait(TE, semx, vx)
                for h in range(4):
                    r.emit(TE, lambda e, h=h, kt=kt, par=par: e.matmul(
                        ptp[par][:, h * 128:(h + 1) * 128],
                        A(PRB[h], kt * 128, kt * 128 + 128),
                        A(DIAG[h]), start=True, stop=True),
                        inc=[(tse, 1)] if h == 3 else None)
                t_pt = tse.n
                if kt % 3 != 2:
                    r.wait(SC, tse, t_pt)
                    r.emit(SC, lambda e, sl=sl, par=par: e.copy(
                        A(R_PTSEG, sl * 512, (sl + 1) * 512), ptp[par][:]),
                        inc=[(sse, 1)])
                    ptp_free[par] = (sse, sse.n)
                else:
                    r.wait(V, tse, t_pt)
                    r.emit(V, lambda e, sl=sl, par=par: e.tensor_copy(
                        A(R_PTSEG, sl * 512, (sl + 1) * 512), ptp[par][:]),
                        inc=[(vse, 1)])
                    ptp_free[par] = (vse, vse.n)
                pt_ready[kt] = ptp_free[par]
            if kt >= 1:
                pk = kt - 1
                semx, vx = pt_ready[pk]
                r.wait(TE, semx, vx)
                r.emit(TE, lambda e, b=b, pk=pk, nkt=nkt: e.matmul(
                    PDp[:],
                    A(R_VSB, b * S + pk * 128, b * S + pk * 128 + 128),
                    A(R_PTSEG, (pk % 8) * 512, (pk % 8) * 512 + 512),
                    start=(pk == 0), stop=(pk == nkt - 1)),
                    inc=[(tse, 1)] if pk == nkt - 1 else None)
        pt_last[0] = tse.n
        r.wait(V, tse, tse.n)
        if t % 8 == 0:
            r.wait(V, yd, yd_stage[0])
        for h in range(4):
            r.emit(V, lambda e, h=h, t=t: e.tensor_copy(
                A(ASTG[h], (t % 8) * 128, (t % 8) * 128 + 128),
                PDp[:, h * 128:(h + 1) * 128]),
                inc=[(vse, 1)] if h == 3 else None)
        pd_free[0] = (vse, vse.n)
        if t % 8 == 3:
            shard = b * 4 + t // 8
            r.wait(SY, vse, vse.n)
            for h in range(4):
                r.emit(SY, lambda e, h=h, shard=shard: e.dma_start(
                    out=a2ix.ap()[shard, h * 128:(h + 1) * 128, :],
                    in_=A(ASTG[h], 0, 512)), inc=[(yd, 16)])
            yd_stage[0] = yd.n
            yd_stx[0] = yd.n
        if t % 8 == 7:
            shard = b * 4 + t // 8
            r.wait(SY, vse, vse.n)
            for h in range(4):
                r.emit(SY, lambda e, h=h, shard=shard: e.dma_start(
                    out=a2iy.ap()[shard, h * 128:(h + 1) * 128, :],
                    in_=A(ASTG[h], 512, 1024)), inc=[(yd, 16)])
            yd_stage[0] = yd.n

    prev = None
    emit_head01(pairs[0][0], pairs[0][1])
    for i, (b, t) in enumerate(pairs):
        if prev is not None:
            emit_pv(prev[0], prev[1], prev[2])
        if i == 61:
            r.wait(G, yd, yd_stx[0])
            r.emit(G, lambda e: e.collective_compute(
                "AllToAll", ALU.bypass, replica_groups=[list(range(NCORES))],
                ins=[a2ix.ap()], outs=[a2ox.ap()]), inc=[(cse, 1)])
            cse_ax[0] = cse.n
        nxt = pairs[i + 1] if i + 1 < len(pairs) else None
        vd = emit_attn_part2(b, t, nxt)
        prev = (b, t, vd)
    emit_pv(prev[0], prev[1], prev[2])
    r.wait(V, tse, tse.n)
    attn_v_end = vse.n
    attn_s_end = sse.n

    yd_a2 = yd.n
    r.wait(G, yd, yd_a2)
    r.emit(G, lambda e: e.collective_compute(
        "AllToAll", ALU.bypass, replica_groups=[list(range(NCORES))],
        ins=[a2iy.ap()], outs=[a2oy.ap()]), inc=[(cse, 1)])
    cse_a2 = cse.n

    r.wait(SY, cse, cse_ax[0])
    for hcg in range(32):
        rr, lr = hcg // 4, hcg % 4
        r.emit(SY, lambda e, hcg=hcg, rr=rr, lr=lr: e.dma_start(
            out=A(R_ATM, hcg * 1024, hcg * 1024 + 512),
            in_=a2ox.ap()[rr, lr * 128:(lr + 1) * 128, :]), inc=[(yd, 16)])
    yd_atm_x = yd.n
    r.wait(SY, cse, cse_a2)
    for hcg in range(32):
        rr, lr = hcg // 4, hcg % 4
        r.emit(SY, lambda e, hcg=hcg, rr=rr, lr=lr: e.dma_start(
            out=A(R_ATM, hcg * 1024 + 512, hcg * 1024 + 1024),
            in_=a2oy.ap()[rr, lr * 128:(lr + 1) * 128, :]), inc=[(yd, 16)])
    yd_atm = yd.n

    r.wait(G, cse, cse_a2)

    F_P1, F_P2, F_PJ = 2048, 2176, 2304
    OPS2 = [[PDp[:], PEp[:], PFp[:], PGp[:]],
            [PAp[:, 0:512], PAp[:, 512:1024], PBp[:, 0:512], PBp[:, 512:1024]]]
    NK = 16
    wo_use = {}
    v_wo = {}
    oq_guard = {}
    v_evac_p = {}
    xr_guard = {}
    first_mm = True
    for nk in range(NK):
        for gq in range(8):
            bo = (gq % 2) * 2048
            gk = nk * 8 + gq
            if gk >= 2:
                r.wait(G, vse, oq_guard[gk - 2])
            r.emit(G, lambda e, nk=nk, gq=gq, bo=bo: e.dma_start(
                out=A(R_OQS, bo, bo + 1024).rearrange("p (g c) -> p g c", g=4),
                in_=oc[gq * 512:(gq + 1) * 512, nk * 256:(nk + 1) * 256].rearrange(
                    "(g p) c -> p g c", p=128)), inc=[(gd, 16)])
            r.emit(G, lambda e, nk=nk, gq=gq, bo=bo: e.dma_start(
                out=A(R_OQS, bo + 1024, bo + 2048).rearrange("p (g c) -> p g c", g=4),
                in_=osr[gq * 512:(gq + 1) * 512, nk * 256:(nk + 1) * 256].rearrange(
                    "(g p) c -> p g c", p=128)), inc=[(gd, 16)])
            r.wait(V, gd, gd.n)
            if nk >= 1:
                r.wait(V, tse, wo_use[gq])
            r.emit(V, lambda e, gq=gq, bo=bo: e.scalar_tensor_tensor(
                out=A(R_WO, gq * 1024, (gq + 1) * 1024), in0=A(R_OQS, bo, bo + 1024),
                scalar=-8.0, in1=A(R_OQS, bo + 1024, bo + 2048),
                op0=ALU.add, op1=ALU.mult), inc=[(vse, 1)])
            for hcg in range(gq * 4, gq * 4 + 4):
                v_wo[(nk, hcg)] = vse.n
            oq_guard[gk] = vse.n
        for p in range(2):
            xb = (p % 2) * 1024
            grp = nk * 2 + p
            if grp < 2:
                r.wait(SY, vse, attn_v_end)
                r.wait(SY, sse, attn_s_end)
            r.wait(SY, vse, xr_guard.get(grp - 2, 0))
            for tt in range(4):
                gtt = p * 4 + tt
                r.emit(SY, lambda e, gtt=gtt, nk=nk, tt=tt, xb=xb: e.dma_start(
                    out=FA(XRS + xb + tt * 256, XRS + xb + (tt + 1) * 256),
                    in_=x_own[gtt * 128:(gtt + 1) * 128, nk * 256:(nk + 1) * 256]),
                    inc=[(yd, 16)])
            yd_x = yd.n
            if first_mm:
                r.wait(TE, yd, yd_atm_x)
                first_mm = False
            if grp == 1:
                r.wait(TE, yd, yd_atm)
            par = grp % 2
            r.wait(TE, vse, v_evac_p.get(grp - 2, 0))
            for hcg in range(32):
                r.wait(TE, vse, v_wo[(nk, hcg)])
                for tt in range(4):
                    r.emit(TE, lambda e, hcg=hcg, p=p, tt=tt, par=par: e.matmul(
                        OPS2[par][tt][:, 0:256],
                        A(R_ATM, hcg * 1024 + p * 512 + tt * 128,
                          hcg * 1024 + p * 512 + tt * 128 + 128),
                        A(R_WO, hcg * 256, (hcg + 1) * 256),
                        start=(hcg == 0), stop=(hcg == 31)),
                        inc=[(tse, 1)] if tt == 3 else None)
                wo_use[hcg // 4] = tse.n
            r.wait(V, tse, tse.n)
            r.wait(V, yd, yd_x)
            for tt in range(4):
                gtt = p * 4 + tt
                r.emit(V, lambda e, gtt=gtt, nk=nk, tt=tt, xb=xb, par=par: e.scalar_tensor_tensor(
                    out=A(R_HPRE, gtt * 4096 + nk * 256, gtt * 4096 + (nk + 1) * 256),
                    in0=OPS2[par][tt][:, 0:256], scalar=0.25,
                    in1=FA(XRS + xb + tt * 256, XRS + xb + (tt + 1) * 256),
                    op0=ALU.mult, op1=ALU.add,
                    accum_out=FA(F_P1 + gtt * 16 + nk, F_P1 + gtt * 16 + nk + 1)),
                    inc=[(vse, 1)] if tt == 3 else None)
            for tt in range(4):
                gtt = p * 4 + tt
                r.emit(V, lambda e, gtt=gtt, nk=nk: e.scalar_tensor_tensor(
                    out=FA(F_PJ, F_PJ + 256),
                    in0=A(R_HPRE, gtt * 4096 + nk * 256, gtt * 4096 + (nk + 1) * 256),
                    scalar=1.0,
                    in1=A(R_HPRE, gtt * 4096 + nk * 256, gtt * 4096 + (nk + 1) * 256),
                    op0=ALU.mult, op1=ALU.mult,
                    accum_out=FA(F_P2 + gtt * 16 + nk, F_P2 + gtt * 16 + nk + 1)))
            v_evac_p[grp] = vse.n
            xr_guard[grp] = vse.n
    v_eall = vse.n


    def emit_rsqrt_cols(vpe_c, rstd_c, nt1_c, nt2_c, nt3_c, w=1):
        vpe, rstd = st(vpe_c, w), st(rstd_c, w)
        n1, n2, n3 = st(nt1_c, w), st(nt2_c, w), st(nt3_c, w)
        r.emit(V, lambda e: e.tensor_scalar(
            out=n1.bitcast(I32), in0=vpe.bitcast(I32), scalar1=1,
            scalar2=None, op0=ALU.arith_shift_right))
        spacer()
        r.emit(V, lambda e: e.tensor_scalar(
            out=rstd.bitcast(I32), in0=n1.bitcast(I32), scalar1=-1,
            scalar2=0x5F3759DF, op0=ALU.mult, op1=ALU.add))
        spacer()
        for _ in range(3):
            r.emit(V, lambda e: e.tensor_tensor(out=n1, in0=rstd, in1=rstd, op=ALU.mult))
            spacer()
            r.emit(V, lambda e: e.tensor_tensor(out=n2, in0=vpe, in1=n1, op=ALU.mult))
            spacer()
            r.emit(V, lambda e: e.tensor_scalar(
                out=n3, in0=n2, scalar1=-0.5, scalar2=1.5, op0=ALU.mult, op1=ALU.add))
            spacer()
            r.emit(V, lambda e: e.tensor_tensor(out=rstd, in0=rstd, in1=n3, op=ALU.mult))
            spacer()

    yd_out = [0]
    r.wait(V, vse, v_eall)

    def hp(a, b, tt):
        return A(R_HPRE, tt * 4096 + a, tt * 4096 + b)

    def hh(a, b, tt):
        return A(R_H, tt * 4096 + a, tt * 4096 + b)

    for tt in range(8):
        r.emit(V, lambda e, tt=tt: e.reduce_sum(
            st(B_S1 + tt), FA(F_P1 + tt * 16, F_P1 + tt * 16 + 16),
            axis=mybir.AxisListType.X))
        r.emit(V, lambda e, tt=tt: e.reduce_sum(
            st(B_S2 + tt), FA(F_P2 + tt * 16, F_P2 + tt * 16 + 16),
            axis=mybir.AxisListType.X))
    spacer(2)
    INV_N = 1.0 / 4096.0
    r.emit(V, lambda e: e.tensor_scalar(
        out=st(B_MEAN, 8), in0=st(B_S1, 8), scalar1=INV_N, scalar2=None, op0=ALU.mult))
    r.emit(V, lambda e: e.tensor_scalar(
        out=st(B_EVAR, 8), in0=st(B_S2, 8), scalar1=INV_N, scalar2=float(EPS),
        op0=ALU.mult, op1=ALU.add))
    r.emit(V, lambda e: e.tensor_scalar(
        out=st(B_NMEAN, 8), in0=st(B_MEAN, 8), scalar1=-1.0, scalar2=None, op0=ALU.mult))
    r.emit(V, lambda e: e.tensor_tensor(
        out=st(B_MSQ, 8), in0=st(B_MEAN, 8), in1=st(B_MEAN, 8), op=ALU.mult))
    spacer()
    r.emit(V, lambda e: e.tensor_tensor(
        out=st(B_VPE, 8), in0=st(B_EVAR, 8), in1=st(B_MSQ, 8), op=ALU.subtract))
    spacer()
    emit_rsqrt_cols(B_VPE, B_RSTD, B_S1, B_S2, B_MSQ, w=8)
    r.emit(V, lambda e: e.tensor_tensor(
        out=st(B_NB, 8), in0=st(B_NMEAN, 8), in1=st(B_RSTD, 8), op=ALU.mult))
    spacer(2)
    r.emit(V, lambda e: e.nop(), inc=[(vse, 1)])

    LG = [PFp, PGp]
    v_lgread = {}
    t_lg_map = {}
    s_h_map = {}
    exp_sse_f = {}

    def emit_router(tt):
        lgb = LG[tt % 2]
        r.emit(V, lambda e, tt=tt: e.tensor_scalar(
            out=hh(0, 4096, tt), in0=hp(0, 4096, tt), scalar1=st(B_RSTD + tt),
            scalar2=st(B_NB + tt), op0=ALU.mult, op1=ALU.add), inc=[(vse, 1)])
        v_h = vse.n
        r.wait(SC, vse, v_h)
        r.emit(SC, lambda e: e.nop(), inc=[(sse, 1)])
        s_h_map[tt] = sse.n
        if tt >= 2:
            r.wait(TE, vse, v_lgread[tt - 2])
        ht_ready = {}
        for hc in range(34):
            if hc >= 2:
                mm = hc - 2
                slot = mm % 2
                r.wait(TE, sse, ht_ready[mm])
                r.emit(TE, lambda e, mm=mm, slot=slot, lgb=lgb: e.matmul(
                    lgb[:, 0:E], A(R_HT, slot * 128, (slot + 1) * 128),
                    wrt_sb[:, mm * E:(mm + 1) * E],
                    start=(mm == 0), stop=False), inc=[(tse, 1)])
            if hc < 32:
                slot = hc % 2
                r.wait(TE, sse, s_h_map[tt])
                if hc >= 2:
                    r.wait(TE, sse, ht_ready[hc - 2])
                pp = PEp if slot == 0 else PDp
                r.emit(TE, lambda e, hc=hc, pp=pp, tt=tt: e.transpose(
                    pp[:, 0:64].bitcast(BF16),
                    hh(hc * 128, (hc + 1) * 128, tt), eye_bf[:]), inc=[(tse, 1)])
                t_tr = tse.n
                r.wait(SC, tse, t_tr)
                r.emit(SC, lambda e, slot=slot, pp=pp: e.copy(
                    A(R_HT, slot * 128, (slot + 1) * 128),
                    pp[:, 0:64].bitcast(BF16)), inc=[(sse, 1)])
                ht_ready[hc] = sse.n
        r.emit(TE, lambda e, lgb=lgb: e.matmul(
            lgb[:, 0:E], one_bf[:], rb_bf[:], start=False, stop=True), inc=[(tse, 1)])
        t_lg_map[tt] = tse.n

    def emit_soft_head(tt):
        lgb = LG[tt % 2]
        r.wait(V, tse, t_lg_map[tt])
        if tt >= 1:
            r.wait(V, sse, exp_sse_f[tt - 1])
        r.emit(V, lambda e, lgb=lgb: e.tensor_scalar(
            out=lg_sb[:], in0=lgb[:, 0:E], scalar1=1.0, scalar2=FMIN,
            op0=ALU.mult, op1=ALU.max, accum_out=st(C_M64)), inc=[(vse, 1)])
        v_lgread[tt] = vse.n
        spacer()
        r.emit(V, lambda e: e.tensor_scalar(
            out=st(C_NM64), in0=st(C_M64), scalar1=-1.0, scalar2=None,
            op0=ALU.mult), inc=[(vse, 1)])
        r.wait(SC, vse, vse.n)
        r.emit(SC, lambda e: e.activation(
            ex_sb[:], lg_sb[:], AF.Exp, bias=st(C_NM64), scale=1.0), inc=[(sse, 1)])
        exp_sse_f[tt] = sse.n

    def emit_soft_tail(tt):
        r.wait(V, sse, exp_sse_f[tt])
        r.emit(V, lambda e: e.tensor_scalar(
            out=cur_sb[:, 0:E], in0=ex_sb[:], scalar1=1.0, scalar2=0.0,
            op0=ALU.mult, op1=ALU.add, accum_out=st(C_SUM0)))
        r.emit(V, lambda e: e.memset(st(C_WS), 0.0))
        for j in range(8):
            ca = (j % 2) * E
            cb = ((j + 1) % 2) * E
            r.emit(V, lambda e, ca=ca: e.reduce_max(
                st(C_MJ), cur_sb[:, ca:ca + E], axis=mybir.AxisListType.X))
            spacer(2)
            r.emit(V, lambda e: e.tensor_scalar(
                out=st(C_MJC), in0=st(C_MJ), scalar1=0.0, scalar2=None,
                op0=ALU.max))
            if j < 7:
                r.emit(V, lambda e, ca=ca: e.tensor_scalar(
                    out=msk_sb[:], in0=cur_sb[:, ca:ca + E], scalar1=st(C_MJ),
                    scalar2=None, op0=ALU.is_ge))
                r.emit(V, lambda e: e.tensor_tensor(
                    out=st(C_WS), in0=st(C_WS), in1=st(C_MJC), op=ALU.add))
                r.emit(V, lambda e, ca=ca, cb=cb: e.scalar_tensor_tensor(
                    out=cur_sb[:, cb:cb + E], in0=msk_sb[:], scalar=-1e30,
                    in1=cur_sb[:, ca:ca + E], op0=ALU.mult, op1=ALU.add))
                spacer()
            else:
                spacer()
                r.emit(V, lambda e: e.tensor_tensor(
                    out=st(C_WS), in0=st(C_WS), in1=st(C_MJC), op=ALU.add))
        r.emit(V, lambda e: e.tensor_scalar(
            out=msk_sb[:], in0=ex_sb[:], scalar1=st(C_MJ), scalar2=None,
            op0=ALU.is_ge))
        r.emit(V, lambda e: e.tensor_tensor(
            out=st(C_COEF), in0=st(C_SUM0), in1=st(C_WS), op=ALU.add))
        r.emit(V, lambda e: e.tensor_tensor(
            out=A(R_WDS), in0=ex_sb[:], in1=msk_sb[:], op=ALU.mult),
            inc=[(vse, 1)])
        v_wd = vse.n
        r.wait(TE, vse, v_wd)
        r.emit(TE, lambda e: e.transpose(
            PAp[0:E, 0:64].bitcast(BF16), A(R_WDS), eye_bf[:]), inc=[(tse, 1)])
        r.wait(SC, tse, tse.n)
        r.emit(SC, lambda e: e.copy(
            ABF[0:E, 75072:75072 + 128], PAp[0:E, 0:64].bitcast(BF16)),
            inc=[(sse, 1)])
        s_wdt = sse.n
        r.wait(TE, sse, s_wdt)
        v_gun = {}
        for nb4 in range(4):
            ps = PAp if nb4 % 2 == 0 else PBp
            if nb4 >= 2:
                r.wait(TE, vse, v_gun[nb4 - 2])
            for half in range(2):
                r.emit(TE, lambda e, ps=ps, nb4=nb4, half=half: e.matmul(
                    ps[:, half * 512:(half + 1) * 512], ABF[0:E, 75072:75072 + 128],
                    eb_sb[:, nb4 * 1024 + half * 512:nb4 * 1024 + (half + 1) * 512],
                    start=True, stop=True), inc=[(tse, 1)] if half == 1 else None)
            r.wait(V, tse, tse.n)
            r.emit(V, lambda e, ps=ps, nb4=nb4, tt=tt: e.scalar_tensor_tensor(
                out=FA(GUN + nb4 * 1024, GUN + (nb4 + 1) * 1024),
                in0=hh(nb4 * 1024, (nb4 + 1) * 1024, tt), scalar=st(C_COEF),
                in1=ps[:], op0=ALU.mult, op1=ALU.add, accum_out=st(C_P1 + nb4)))
            r.emit(V, lambda e, nb4=nb4: e.scalar_tensor_tensor(
                out=FA(OUTB, OUTB + 1024),
                in0=FA(GUN + nb4 * 1024, GUN + (nb4 + 1) * 1024), scalar=1.0,
                in1=FA(GUN + nb4 * 1024, GUN + (nb4 + 1) * 1024),
                op0=ALU.mult, op1=ALU.mult, accum_out=st(C_P2 + nb4)),
                inc=[(vse, 1)])
            v_gun[nb4] = vse.n
        r.emit(V, lambda e: e.reduce_sum(
            st(C_SUM), st(C_P1, 4), axis=mybir.AxisListType.X))
        r.emit(V, lambda e: e.reduce_sum(
            st(C_SSQ), st(C_P2, 4), axis=mybir.AxisListType.X))
        spacer(2)
        r.emit(V, lambda e: e.tensor_scalar(
            out=st(C_MEAN), in0=st(C_SUM), scalar1=INV_N, scalar2=None, op0=ALU.mult))
        r.emit(V, lambda e: e.tensor_scalar(
            out=st(C_EVAR), in0=st(C_SSQ), scalar1=INV_N, scalar2=float(EPS),
            op0=ALU.mult, op1=ALU.add))
        r.emit(V, lambda e: e.tensor_tensor(
            out=st(C_MSQ), in0=st(C_MEAN), in1=st(C_MEAN), op=ALU.mult))
        spacer()
        r.emit(V, lambda e: e.tensor_tensor(
            out=st(C_VPE), in0=st(C_EVAR), in1=st(C_MSQ), op=ALU.subtract))
        spacer()
        emit_rsqrt_cols(C_VPE, C_RSTD, C_NT1, C_NT2, C_NT3)
        r.emit(V, lambda e: e.tensor_scalar(
            out=st(C_NB), in0=st(C_MEAN), scalar1=st(C_RSTD), scalar2=-1.0,
            op0=ALU.mult, op1=ALU.mult))
        spacer()
        r.wait(V, yd, yd_out[0])
        r.emit(V, lambda e: e.tensor_scalar(
            out=FA(OUTB, OUTB + 4096), in0=FA(GUN, GUN + 4096),
            scalar1=st(C_RSTD), scalar2=st(C_NB), op0=ALU.mult, op1=ALU.add),
            inc=[(vse, 1)])
        r.wait(SY, vse, vse.n)
        r.emit(SY, lambda e, tt=tt: e.dma_start(
            out=out[tt * 128:(tt + 1) * 128, :], in_=FA(OUTB, OUTB + 4096)),
            inc=[(yd, 16)])
        yd_out[0] = yd.n

    emit_router(0)
    for tt in range(8):
        emit_soft_head(tt)
        if tt + 1 < 8:
            emit_router(tt + 1)
        emit_soft_tail(tt)

    r.wait(SY, yd, yd.n)

    with nc.Block() as blk:
        @blk.tensor
        def _(e):
            r.play("tensor", e)

        @blk.vector
        def _(e):
            r.play("vector", e)

        @blk.scalar
        def _(e):
            r.play("scalar", e)

        @blk.gpsimd
        def _(e):
            r.play("gpsimd", e)

        @blk.sync
        def _(e):
            r.play("sync", e)

    ctx.close()
    return nc


_NC_CACHE = [None]


def kernel(L0_attn_x, L0_attn_q_codes, L0_attn_k_codes, L0_attn_v_codes,
           L0_attn_o_codes, L0_attn_q_scales, L0_attn_k_scales,
           L0_attn_v_scales, L0_attn_o_scales, L0_ffn_router_weight,
           L0_ffn_router_bias, L0_ffn_expert_bias):
    xf = np.ascontiguousarray(np.asarray(L0_attn_x, np.float32).reshape(T, DM))
    qcod = np.ascontiguousarray(np.asarray(L0_attn_q_codes, np.int32))
    kcod = np.ascontiguousarray(np.asarray(L0_attn_k_codes, np.int32))
    vcod = np.ascontiguousarray(np.asarray(L0_attn_v_codes, np.int32))
    ocod = np.ascontiguousarray(np.asarray(L0_attn_o_codes, np.int32))

    def rep(s):
        return np.ascontiguousarray(np.repeat(np.asarray(s, np.float32), 128, axis=0))

    qsr, ksr, vsr, osrr = (rep(s) for s in (
        L0_attn_q_scales, L0_attn_k_scales, L0_attn_v_scales, L0_attn_o_scales))
    rww = np.ascontiguousarray(np.asarray(L0_ffn_router_weight, np.float32))
    rbb = np.ascontiguousarray(np.asarray(L0_ffn_router_bias, np.float32).reshape(1, E))
    ebb = np.ascontiguousarray(np.asarray(L0_ffn_expert_bias, np.float32))

    half = 64
    inv = (1.0 / (10000.0 ** (np.arange(half, dtype=np.float32) / np.float32(half)))).astype(np.float32)
    pos = np.arange(S, dtype=np.float32)
    ang = inv[:, None] * pos[None, :]
    ang2 = np.concatenate([ang, ang], axis=0).astype(np.float32)
    cosT = np.cos(ang2).astype(np.float32)
    sinT = np.sin(ang2).astype(np.float32)
    sinN = np.concatenate([-sinT[0:64], sinT[64:128]], 0)
    eye = np.eye(128, dtype=np.float32)
    rotMT = np.zeros((128, 128), np.float32)
    for d in range(64):
        rotMT[d + 64, d] = -1.0
        rotMT[d, d + 64] = 1.0
    tri = np.triu(np.full((128, 128), np.float32(-1.0e30), np.float32), k=1)
    ones1 = np.ones((1, 128), np.float32)

    xtf = np.ascontiguousarray(xf.T)
    in_maps = []
    for i in range(NCORES):
        in_maps.append(dict(
            x_own=np.ascontiguousarray(xf[i * TPC:(i + 1) * TPC]),
            xt=xtf,
            qc=np.ascontiguousarray(qcod[:, i * QSL:(i + 1) * QSL]),
            kc=np.ascontiguousarray(kcod[:, i * D:(i + 1) * D]),
            vc=np.ascontiguousarray(vcod[:, i * D:(i + 1) * D]),
            oc=ocod,
            qs=np.ascontiguousarray(qsr[:, i * QSL:(i + 1) * QSL]),
            ks=np.ascontiguousarray(ksr[:, i * D:(i + 1) * D]),
            vs=np.ascontiguousarray(vsr[:, i * D:(i + 1) * D]),
            osr=osrr,
            rw=rww, rb=rbb, ebi=ebb,
            cosc=cosT, sinc=sinN, eyec=eye, tric=tri, onec=ones1,
        ))

    if _NC_CACHE[0] is None:
        with allow_ici_all_to_all():
            _NC_CACHE[0] = build_nc()
    res = run_bass_kernel_spmd(_NC_CACHE[0], in_maps, core_ids=list(range(NCORES)))
    full = np.concatenate([res.results[i]["out"] for i in range(NCORES)], axis=0)
    return full.reshape(B, S, DM).astype(np.float32)


# version 23
# speedup vs baseline: 1.1655x; 1.1655x over previous
import numpy as np
from contextlib import ExitStack
import concourse.bass as bass
import concourse.mybir as mybir
from concourse.bass_utils import run_bass_kernel_spmd
from concourse.aot_env import allow_ici_all_to_all

F32 = mybir.dt.float32
BF16 = mybir.dt.bfloat16
I32 = mybir.dt.int32
AF = mybir.ActivationFunctionType
ALU = mybir.AluOpType

NCORES = 8
B, S, DM = 2, 4096, 4096
T = B * S
TPC = T // NCORES
D = 128
QSL = 512
E = 64
SCALE = float((1.0 / np.sqrt(np.float32(128.0))) / 4.0)
EPS = 1e-5
FMIN = -3.0e38

ENGS = ("tensor", "vector", "scalar", "gpsimd", "sync")
TE, V, SC, G, SY = ENGS


class Sem:
    def __init__(self, handle):
        self.h = handle
        self.n = 0


class Rec:
    def __init__(self):
        self.ops = {e: [] for e in ENGS}

    def emit(self, eng, fn, inc=None):
        incs = []
        if inc:
            for s, by in inc:
                s.n += by
                incs.append((s, by))
        self.ops[eng].append(("op", fn, incs))

    def wait(self, eng, sem, val=None):
        v = sem.n if val is None else val
        if v <= 0:
            return
        lst = self.ops[eng]
        if lst and lst[-1][0] == "wait" and lst[-1][1] is sem:
            if v > lst[-1][2]:
                lst[-1] = ("wait", sem, v)
            return
        lst.append(("wait", sem, v))

    def play(self, eng_name, e):
        for kind, a, b in self.ops[eng_name]:
            if kind == "wait":
                e.wait_ge(a.h, b)
            else:
                ins = a(e)
                for s, by in b:
                    ins.then_inc(s.h, by)


def build_nc():
    nc = bass.Bass("TRN2", target_bir_lowering=False, debug=False)

    x_own = nc.dram_tensor("x_own", [TPC, DM], F32, kind="ExternalInput").ap()
    xt = nc.dram_tensor("xt", [DM, T], F32, kind="ExternalInput").ap()
    qc = nc.dram_tensor("qc", [DM, QSL], I32, kind="ExternalInput").ap()
    kc = nc.dram_tensor("kc", [DM, D], I32, kind="ExternalInput").ap()
    vc = nc.dram_tensor("vc", [DM, D], I32, kind="ExternalInput").ap()
    oc = nc.dram_tensor("oc", [DM, DM], I32, kind="ExternalInput").ap()
    qs = nc.dram_tensor("qs", [DM, QSL], F32, kind="ExternalInput").ap()
    ks = nc.dram_tensor("ks", [DM, D], F32, kind="ExternalInput").ap()
    vs = nc.dram_tensor("vs", [DM, D], F32, kind="ExternalInput").ap()
    osr = nc.dram_tensor("osr", [DM, DM], F32, kind="ExternalInput").ap()
    rw = nc.dram_tensor("rw", [E, DM], F32, kind="ExternalInput").ap()
    rb = nc.dram_tensor("rb", [1, E], F32, kind="ExternalInput").ap()
    ebi = nc.dram_tensor("ebi", [E, DM], F32, kind="ExternalInput").ap()
    cosc = nc.dram_tensor("cosc", [128, S], F32, kind="ExternalInput").ap()
    sinc = nc.dram_tensor("sinc", [128, S], F32, kind="ExternalInput").ap()
    eyec = nc.dram_tensor("eyec", [128, 128], F32, kind="ExternalInput").ap()
    tric = nc.dram_tensor("tric", [128, 128], F32, kind="ExternalInput").ap()
    onec = nc.dram_tensor("onec", [1, 128], F32, kind="ExternalInput").ap()
    out = nc.dram_tensor("out", [TPC, DM], F32, kind="ExternalOutput").ap()

    a2ix = nc.dram_tensor("a2ix", [NCORES, QSL, TPC // 2], BF16)
    a2iy = nc.dram_tensor("a2iy", [NCORES, QSL, TPC // 2], BF16)
    a2ox = nc.dram_tensor("a2ox", [NCORES, QSL, TPC // 2], BF16)
    a2oy = nc.dram_tensor("a2oy", [NCORES, QSL, TPC // 2], BF16)

    ctx = ExitStack()
    sb = lambda name, shape, dt: ctx.enter_context(nc.sbuf_tensor(name, shape, dt))

    ABF = sb("abf", [128, 77824], BF16)
    AFR = sb("afr", [128, 8192], F32)
    eye_bf = sb("eye_bf", [128, 128], BF16)
    one_bf = sb("one_bf", [1, 128], BF16)
    rb_bf = sb("rb_bf", [1, E], BF16)
    tri_f = sb("tri_f", [128, 128], F32)
    wrt_sb = sb("wrt_sb", [128, 32 * E], BF16)
    eb_sb = sb("eb_sb", [E, DM], BF16)
    rw_bf = sb("rw_bf", [E, DM], BF16)
    stat = sb("stat", [128, 104], F32)
    lg_sb = sb("lg_sb", [128, E], F32)
    ex_sb = sb("ex_sb", [128, E], F32)
    cur_sb = sb("cur_sb", [128, 2 * E], F32)
    msk_sb = sb("msk_sb", [128, E], F32)

    def A(role, a=0, b=None):
        off, w = role
        if b is None:
            b = w
        assert 0 <= a <= b <= w, (role, a, b)
        return ABF[:, off + a: off + b]

    def FA(a, b):
        return AFR[:, a:b]

    R_QROPE = (0, 4 * T)
    R_KROPE = (32768, T)
    R_VSB = (40960, T)
    R_WQ = (49152, 32 * 512)
    R_WK = (65536, 32 * 128)
    R_WV = (69632, 32 * 128)
    R_DQS = (73728, 4096)
    R_XTS = (73728, 2 * 2048)
    R_XTOWN = (0, 32 * 1024)
    R_XBF = (32768, 2 * 4096)
    R_KVST = (40960, 8192)
    R_COS = (49152, 1024)
    R_SIN = (50176, 1024)
    R_RT1 = (51200, 1024)
    R_RT2 = (52224, 1024)
    PRB = [(49152 + h * 4096, 4096) for h in range(4)]
    R_PTSEG = (65536, 4096)
    ASTG = [(69632 + h * 1024, 1024) for h in range(4)]
    DIAG = [(73728 + h * 128, 128) for h in range(4)]
    R_JUNK = (74240, 2048)
    DIAG2 = [(76288 + h * 128, 128) for h in range(4)]
    R_ATM = (0, 32 * 1024)
    R_WO = (32768, 32 * 256)
    R_HPRE = (40960, 8 * 4096)
    R_OQS = (73728, 4096)
    R_H = (0, 8 * 4096)
    R_HT = (74752, 256)
    R_WDS = (75008, 64)
    R_WDT = (75072, 128)

    SRA, SRB = 0, 4096
    XRS = 0
    GUN = 0
    OUTB = 4096

    MXS = [[h * 4 + s for s in range(4)] for h in range(4)]
    M01 = [16 + h for h in range(4)]
    M23 = [20 + h for h in range(4)]
    MXC = [24 + h for h in range(4)]
    NMX = [28 + h for h in range(4)]
    SMS = [32 + h for h in range(4)]
    RCP = [36 + h for h in range(4)]
    C_SUM, C_SSQ, C_MEAN, C_EVAR, C_MSQ, C_VPE, C_RSTD, C_NB = 0, 1, 2, 3, 4, 5, 6, 7
    C_NT1, C_NT2, C_NT3 = 8, 9, 10
    C_M64, C_NM64, C_SUM0, C_WS, C_MJ, C_COEF = 11, 12, 13, 14, 15, 16
    C_P1, C_P2 = 96, 100
    C_SP1, C_SP2, C_MJC = 17, 18, 19
    B_S1, B_S2, B_MEAN, B_NMEAN, B_EVAR, B_MSQ, B_VPE, B_RSTD, B_NB = (
        40, 48, 56, 64, 72, 80, 88, 20, 28)
    C_BN = 32

    def st(c, w=1):
        return stat[:, c:c + w]

    PAp = ctx.enter_context(nc.psum_tensor("pa", [128, 1024], F32))
    PBp = ctx.enter_context(nc.psum_tensor("pb", [128, 1024], F32))
    PDp = ctx.enter_context(nc.psum_tensor("pd", [128, 512], F32))
    PEp = ctx.enter_context(nc.psum_tensor("pe", [128, 512], F32))
    PFp = ctx.enter_context(nc.psum_tensor("pf", [128, 512], F32))
    PGp = ctx.enter_context(nc.psum_tensor("pg", [128, 512], F32))

    sems = {k: Sem(ctx.enter_context(nc.semaphore(k)))
            for k in ("gd", "yd", "ts", "vs", "ss", "cs")}
    gd, yd, tse, vse, sse, cse = (sems[k] for k in ("gd", "yd", "ts", "vs", "ss", "cs"))

    r = Rec()

    def spacer(n=1):
        for _ in range(n):
            r.emit(V, lambda e: e.tensor_scalar(
                out=st(C_SP1), in0=st(C_SP2), scalar1=1.0, scalar2=None,
                op0=ALU.mult))

    r.emit(SC, lambda e: e.preload_act_table(AF.Exp))
    r.emit(G, lambda e: e.dma_start(out=eye_bf[:], in_=eyec), inc=[(gd, 16)])
    r.emit(G, lambda e: e.dma_start(out=one_bf[:], in_=onec), inc=[(gd, 16)])
    r.emit(G, lambda e: e.dma_start(out=rb_bf[:], in_=rb), inc=[(gd, 16)])
    r.emit(G, lambda e: e.dma_start(out=tri_f[:], in_=tric), inc=[(gd, 16)])
    r.emit(G, lambda e: e.dma_start(out=eb_sb[:], in_=ebi), inc=[(gd, 16)])
    r.emit(G, lambda e: e.dma_start(out=rw_bf[:], in_=rw), inc=[(gd, 16)])
    gd_consts = gd.n

    r.wait(TE, gd, gd_consts)

    vse_wq = {}
    for gq in range(8):
        if gq >= 1:
            r.wait(G, vse, vse_wq[gq - 1])
        r.emit(G, lambda e, gq=gq: e.dma_start(
            out=A(R_DQS, 0, 2048).rearrange("p (g c) -> p g c", g=4),
            in_=qc[gq * 512:(gq + 1) * 512, :].rearrange("(g p) c -> p g c", p=128)),
            inc=[(gd, 16)])
        r.emit(G, lambda e, gq=gq: e.dma_start(
            out=A(R_DQS, 2048, 4096).rearrange("p (g c) -> p g c", g=4),
            in_=qs[gq * 512:(gq + 1) * 512, :].rearrange("(g p) c -> p g c", p=128)),
            inc=[(gd, 16)])
        r.wait(V, gd, gd.n)
        r.emit(V, lambda e, gq=gq: e.scalar_tensor_tensor(
            out=A(R_WQ, gq * 2048, (gq + 1) * 2048), in0=A(R_DQS, 0, 2048),
            scalar=-8.0, in1=A(R_DQS, 2048, 4096), op0=ALU.add, op1=ALU.mult),
            inc=[(vse, 1)])
        vse_wq[gq] = vse.n
    r.emit(G, lambda e: e.dma_start(
        out=A(R_KVST, 0, 4096).rearrange("p (g c) -> p g c", g=32), in_=kc.rearrange("(g p) c -> p g c", p=128)),
        inc=[(gd, 16)])
    r.emit(G, lambda e: e.dma_start(
        out=A(R_KVST, 4096, 8192).rearrange("p (g c) -> p g c", g=32), in_=ks.rearrange("(g p) c -> p g c", p=128)),
        inc=[(gd, 16)])
    r.wait(V, gd, gd.n)
    r.emit(V, lambda e: e.scalar_tensor_tensor(
        out=A(R_WK), in0=A(R_KVST, 0, 4096), scalar=-8.0,
        in1=A(R_KVST, 4096, 8192), op0=ALU.add, op1=ALU.mult), inc=[(vse, 1)])
    v_wk = vse.n
    r.wait(G, vse, v_wk)
    r.emit(G, lambda e: e.dma_start(
        out=A(R_KVST, 0, 4096).rearrange("p (g c) -> p g c", g=32), in_=vc.rearrange("(g p) c -> p g c", p=128)),
        inc=[(gd, 16)])
    r.emit(G, lambda e: e.dma_start(
        out=A(R_KVST, 4096, 8192).rearrange("p (g c) -> p g c", g=32), in_=vs.rearrange("(g p) c -> p g c", p=128)),
        inc=[(gd, 16)])
    r.wait(V, gd, gd.n)
    r.emit(V, lambda e: e.scalar_tensor_tensor(
        out=A(R_WV), in0=A(R_KVST, 0, 4096), scalar=-8.0,
        in1=A(R_KVST, 4096, 8192), op0=ALU.add, op1=ALU.mult), inc=[(vse, 1)])
    vse_wall = vse.n

    sse_wrt = {}
    for hc in range(32):
        pp = PDp if hc % 2 == 0 else PEp
        if hc >= 2:
            r.wait(TE, sse, sse_wrt[hc - 2])
        r.emit(TE, lambda e, hc=hc, pp=pp: e.transpose(
            pp[:, 0:32].bitcast(BF16),
            rw_bf[:, hc * 128:(hc + 1) * 128], eye_bf[0:E, 0:E]), inc=[(tse, 1)])
        r.wait(SC, tse, tse.n)
        r.emit(SC, lambda e, hc=hc, pp=pp: e.copy(
            wrt_sb[:, hc * E:(hc + 1) * E],
            pp[:, 0:32].bitcast(BF16)),
            inc=[(sse, 1)])
        sse_wrt[hc] = sse.n

    def wsl(dt, hc):
        if dt < 4:
            return A(R_WQ, hc * 512 + dt * 128, hc * 512 + dt * 128 + 128)
        if dt == 4:
            return A(R_WK, hc * 128, hc * 128 + 128)
        return A(R_WV, hc * 128, hc * 128 + 128)

    def rawsl2(dt, tc, a, b):
        if dt < 4:
            return A(R_QROPE, dt * T + tc * 1024 + a, dt * T + tc * 1024 + b)
        if dt == 4:
            return A(R_KROPE, tc * 1024 + a, tc * 1024 + b)
        return A(R_VSB, tc * 1024 + a, tc * 1024 + b)

    def rawsl(dt, tc):
        if dt < 4:
            return A(R_QROPE, dt * T + tc * 1024, dt * T + (tc + 1) * 1024)
        if dt == 4:
            return A(R_KROPE, tc * 1024, (tc + 1) * 1024)
        return A(R_VSB, tc * 1024, (tc + 1) * 1024)

    sse_proj = {}
    tse_xuse = {}

    def ptgt(dt):
        if dt == 0: return PAp[:, 0:512]
        if dt == 1: return PAp[:, 512:1024]
        if dt == 2: return PBp[:, 0:512]
        if dt == 3: return PBp[:, 512:1024]
        if dt == 4: return PDp[:]
        return PEp[:]

    F_COS = [0, 1024]
    F_SINN = [2048, 3072]
    F_VBNC = 4096
    F_SWP3 = [4608, 5120, 5632]
    F_RT1 = 6144
    F_RT2 = 7168
    gd_cs = {}
    vse_rope = {}
    sse_vt = {}
    vt_tse = {}

    def srcp(tc, j, a, b, p0=0, p1=128):
        if j < 4:
            base = R_QROPE[0] + j * T + tc * 1024
        else:
            base = R_KROPE[0] + tc * 1024
        return ABF[p0:p1, base + a:base + b]

    def emit_vt(gv):
        if gv >= 1:
            r.wait(V, tse, vt_tse[gv - 1])
        r.wait(V, sse, sse_proj[gv * 2 + 1])
        r.emit(V, lambda e, gv=gv: e.tensor_copy(
            AFR[:, F_VBNC:F_VBNC + 512].bitcast(BF16),
            A(R_VSB, gv * 1024, (gv + 1) * 1024)), inc=[(vse, 1)])
        r.wait(TE, vse, vse.n)
        if gv >= 1:
            r.wait(TE, sse, sse_vt[gv - 1])
        for idx in range(8):
            pp = PFp if idx < 4 else PGp
            r.emit(TE, lambda e, idx=idx, pp=pp: e.transpose(
                pp[:, (idx % 4) * 64:(idx % 4) * 64 + 64].bitcast(BF16),
                AFR[:, F_VBNC + idx * 64:F_VBNC + (idx + 1) * 64].bitcast(BF16),
                eye_bf[:]),
                inc=[(tse, 1)] if idx in (3, 7) else None)
        vt_tse[gv] = tse.n
        r.wait(SC, tse, tse.n)
        r.emit(SC, lambda e, gv=gv: e.copy(
            A(R_VSB, gv * 1024, gv * 1024 + 512), PFp[:, 0:256].bitcast(BF16)))
        r.emit(SC, lambda e, gv=gv: e.copy(
            A(R_VSB, gv * 1024 + 512, gv * 1024 + 1024), PGp[:, 0:256].bitcast(BF16)),
            inc=[(sse, 1)])
        sse_vt[gv] = sse.n

    def emit_rope(tc):
        parc = tc % 2
        pos0 = (tc * 1024) % S
        if tc >= 2:
            r.wait(G, vse, vse_rope[(tc - 2) * 5 + 4])
        r.emit(G, lambda e, pos0=pos0, parc=parc: e.dma_start(
            out=FA(F_COS[parc], F_COS[parc] + 1024),
            in_=cosc[:, pos0:pos0 + 1024]), inc=[(gd, 16)])
        r.emit(G, lambda e, pos0=pos0, parc=parc: e.dma_start(
            out=FA(F_SINN[parc], F_SINN[parc] + 1024),
            in_=sinc[:, pos0:pos0 + 1024]), inc=[(gd, 16)])
        gd_cs[tc] = gd.n
        r.wait(SY, sse, sse_proj[tc * 2 + 1])
        r.wait(V, sse, sse_proj[tc * 2 + 1])
        r.wait(V, gd, gd_cs[tc])
        for j in range(5):
            idx = tc * 5 + j
            sl = F_SWP3[idx % 3]
            if idx >= 3:
                r.wait(SY, vse, vse_rope[idx - 3])
            r.emit(SY, lambda e, tc=tc, j=j, sl=sl: e.dma_start(
                out=AFR[0:64, sl:sl + 512].bitcast(BF16),
                in_=srcp(tc, j, 0, 1024, 64, 128)), inc=[(yd, 16)])
            r.emit(SY, lambda e, tc=tc, j=j, sl=sl: e.dma_start(
                out=AFR[64:128, sl:sl + 512].bitcast(BF16),
                in_=srcp(tc, j, 0, 1024, 0, 64)), inc=[(yd, 16)])
            r.wait(V, yd, yd.n)
            r.emit(V, lambda e, tc=tc, j=j, parc=parc: e.tensor_tensor(
                out=FA(F_RT1, F_RT1 + 1024), in0=srcp(tc, j, 0, 1024),
                in1=FA(F_COS[parc], F_COS[parc] + 1024), op=ALU.mult))
            r.emit(V, lambda e, parc=parc, sl=sl: e.tensor_tensor(
                out=FA(F_RT2, F_RT2 + 1024),
                in0=AFR[:, sl:sl + 512].bitcast(BF16),
                in1=FA(F_SINN[parc], F_SINN[parc] + 1024), op=ALU.mult))
            r.emit(V, lambda e, tc=tc, j=j: e.tensor_tensor(
                out=srcp(tc, j, 0, 1024), in0=FA(F_RT1, F_RT1 + 1024),
                in1=FA(F_RT2, F_RT2 + 1024), op=ALU.add), inc=[(vse, 1)])
            vse_rope[idx] = vse.n

    r.wait(G, vse, vse_wall)
    r.wait(TE, vse, vse_wall)
    r.wait(TE, sse, sse.n)
    for tc in range(8):
        for th in range(2):
            if th == 1 and tc >= 1:
                emit_vt(tc - 1)
            for hc2 in range(16):
                key = (tc * 2 + th) * 16 + hc2
                buf = key % 4
                if key >= 4:
                    r.wait(G, tse, tse_xuse[key - 4])
                r.emit(G, lambda e, tc=tc, th=th, hc2=hc2, buf=buf: e.dma_start(
                    out=A(R_XTS, buf * 1024, buf * 1024 + 1024).rearrange(
                        "p (g c) -> p g c", g=2),
                    in_=xt[hc2 * 256:(hc2 + 1) * 256,
                           tc * 1024 + th * 512:tc * 1024 + th * 512 + 512]
                    .rearrange("(g p) c -> p g c", p=128)), inc=[(gd, 16)])
                r.wait(TE, gd, gd.n)
                if hc2 == 0 and (tc * 2 + th) >= 1:
                    r.wait(TE, sse, sse_proj[tc * 2 + th - 1])
                for g in range(2):
                    hc = hc2 * 2 + g
                    for dt in range(6):
                        last = (dt == 5 and g == 1)
                        r.emit(TE, lambda e, dt=dt, hc=hc, buf=buf, g=g: e.matmul(
                            ptgt(dt),
                            wsl(dt, hc),
                            A(R_XTS, buf * 1024 + g * 512,
                              buf * 1024 + (g + 1) * 512),
                            start=(hc == 0), stop=(hc == 31)),
                            inc=[(tse, 1)] if last else None)
                tse_xuse[key] = tse.n
                if th == 0 and hc2 == 0 and tc >= 1:
                    emit_rope(tc - 1)
            r.wait(SC, tse, tse.n)
            for dt in range(6):
                r.emit(SC, lambda e, dt=dt, tc=tc, th=th: e.copy(
                    rawsl2(dt, tc, th * 512, th * 512 + 512), ptgt(dt)),
                    inc=[(sse, 1)] if dt == 5 else None)
            sse_proj[tc * 2 + th] = sse.n
    emit_vt(7)
    emit_rope(7)
    sse_projall = sse.n
    vse_ropeall = vse.n
    sse_vtall = sse.n

    pairs = [(b, t) for b in range(2) for t in range(32)]
    SCR = [PAp, PBp]
    scr_free = {}
    seg_ctr = [0]
    ptp = [PEp, PFp]
    ptp_free = {}
    pt_last = [0]
    sr_busy = {}
    yd_stage = [0]
    yd_stx = [0]
    cse_ax = [0]
    pd_free = [0]

    def lk(t):
        return (t + 1) * 128

    def prbsl(par, b, h, a, bb):
        if par == 0:
            return A(PRB[h], a, bb)
        base = R_QROPE[0] + h * T + b * S
        return ABF[:, base + a:base + bb]

    def diagp(par, h):
        return A(DIAG[h]) if par == 0 else A(DIAG2[h])

    pt_par = {0: 0, 1: 0}

    r.wait(TE, vse, vse_ropeall)
    r.wait(TE, sse, sse_vtall)

    def emit_head01(b, t):
        emit_head_full(b, t, 0)
        emit_head_full(b, t, 1)
        emit_cmb(t, 0)

    def emit_head_full(b, t, h):
        segs = (lk(t) + 1023) // 1024
        for s in range(segs):
            emit_seg_mm(b, t, h, s)
            emit_seg_evac(b, t, h, s)

    seg_reg = {}

    def emit_seg_mm(b, t, h, s):
        if True:
            a = s * 1024
            bnd = min(lk(t), a + 1024)
            w = bnd - a
            reg = seg_ctr[0] % 2
            seg_ctr[0] += 1
            seg_reg[(h, s)] = reg
            r.wait(TE, vse, scr_free.get(reg, 0))
            for half in range((w + 511) // 512):
                wa = half * 512
                wb = min(w, wa + 512)
                r.emit(TE, lambda e, b=b, t=t, h=h, a=a, wa=wa, wb=wb, reg=reg: e.matmul(
                    SCR[reg][:, wa:wb],
                    A(R_QROPE, h * T + b * S + t * 128, h * T + b * S + t * 128 + 128),
                    A(R_KROPE, b * S + a + wa, b * S + a + wb),
                    start=True, stop=True),
                    inc=[(tse, 1)] if wb == w else None)
            seg_tse[(h, s)] = tse.n

    def emit_seg_evac(b, t, h, s):
        srh = h % 2
        sro = SRA if srh == 0 else SRB
        if True:
            a = s * 1024
            bnd = min(lk(t), a + 1024)
            w = bnd - a
            reg = seg_reg[(h, s)]
            r.wait(V, tse, seg_tse[(h, s)])
            if s == 0:
                r.wait(V, sse, sr_busy.get(srh, 0))
            if a <= t * 128 < bnd:
                dloc = t * 128 - a
                r.emit(V, lambda e, reg=reg, dloc=dloc: e.tensor_tensor(
                    out=SCR[reg][:, dloc:dloc + 128], in0=SCR[reg][:, dloc:dloc + 128],
                    in1=tri_f[:], op=ALU.add))
            r.emit(V, lambda e, reg=reg, w=w, sro=sro, a=a, h=h, s=s: e.tensor_scalar(
                out=FA(sro + a, sro + a + w), in0=SCR[reg][:, 0:w],
                scalar1=SCALE, scalar2=FMIN,
                op0=ALU.mult, op1=ALU.max, accum_out=st(MXS[h][s])),
                inc=[(vse, 1)])
            scr_free[reg] = vse.n

    def emit_cmb(t, h):
        ns = (lk(t) + 1023) // 1024
        if ns == 2:
            r.emit(V, lambda e, h=h: e.tensor_tensor(
                out=st(MXC[h]), in0=st(MXS[h][0]), in1=st(MXS[h][1]), op=ALU.max))
        elif ns == 3:
            r.emit(V, lambda e, h=h: e.tensor_tensor(
                out=st(M01[h]), in0=st(MXS[h][0]), in1=st(MXS[h][1]), op=ALU.max))
            spacer()
            r.emit(V, lambda e, h=h: e.tensor_tensor(
                out=st(MXC[h]), in0=st(M01[h]), in1=st(MXS[h][2]), op=ALU.max))
        elif ns == 4:
            r.emit(V, lambda e, h=h: e.tensor_tensor(
                out=st(M01[h]), in0=st(MXS[h][0]), in1=st(MXS[h][1]), op=ALU.max))
            r.emit(V, lambda e, h=h: e.tensor_tensor(
                out=st(M23[h]), in0=st(MXS[h][2]), in1=st(MXS[h][3]), op=ALU.max))
            spacer()
            r.emit(V, lambda e, h=h: e.tensor_tensor(
                out=st(MXC[h]), in0=st(M01[h]), in1=st(M23[h]), op=ALU.max))

    def mxcol(t, h):
        ns = (lk(t) + 1023) // 1024
        return MXS[h][0] if ns == 1 else MXC[h]

    def emit_nmx(t, h):
        r.emit(V, lambda e, h=h, t=t: e.tensor_scalar(
            out=st(NMX[h]), in0=st(mxcol(t, h)), scalar1=-1.0, scalar2=None,
            op0=ALU.mult), inc=[(vse, 1)])

    def emit_exp(b, t, h, par):
        sro = SRA if h % 2 == 0 else SRB
        r.wait(SC, vse, vse.n)
        r.wait(SC, tse, pt_par[par])
        r.emit(SC, lambda e, h=h, t=t, sro=sro, b=b, par=par: e.activation(
            prbsl(par, b, h, 0, lk(t)), FA(sro, sro + lk(t)), AF.Exp,
            bias=st(NMX[h]), scale=1.0), inc=[(sse, 1)])
        sr_busy[h % 2] = sse.n
        exp_sse[h] = sse.n

    sms_junk = {}
    sms_direct = {}
    exp_sse = {}
    SMB = [40 + h for h in range(4)]

    def emit_cast(b, t, h, par):
        L = lk(t)
        L2 = (L // 256) * 128
        if L2 == 0:
            L2 = L
        r.emit(SC, lambda e, h=h, L2=L2, b=b, par=par: e.activation(
            A(R_JUNK, 0, L2),
            prbsl(par, b, h, 0, L2), AF.Copy, bias=0.0, scale=1.0,
            accum_out=st(SMS[h])), inc=[(sse, 1)])
        sms_junk[h] = sse.n
        sms_direct[h] = (L2 == L)
        if L2 < L:
            r.wait(V, sse, exp_sse[h])
            r.emit(V, lambda e, h=h, L2=L2, L=L, b=b, par=par: e.tensor_scalar(
                out=prbsl(par, b, h, L2, L), in0=prbsl(par, b, h, L2, L),
                scalar1=1.0, scalar2=0.0, op0=ALU.mult, op1=ALU.add,
                accum_out=st(SMB[h])), inc=[(vse, 1)])

    def emit_add(h):
        if sms_direct[h]:
            return
        r.wait(V, sse, sms_junk[h])
        r.emit(V, lambda e, h=h: e.tensor_tensor(
            out=st(SMS[h]), in0=st(SMS[h]), in1=st(SMB[h]), op=ALU.add))

    def emit_recip(h):
        if sms_direct[h]:
            r.wait(V, sse, sms_junk[h])
        r.emit(V, lambda e, h=h: e.reciprocal(st(RCP[h]), st(SMS[h])))

    def emit_diag(h, par, last=False):
        r.wait(V, tse, pt_par[par])
        r.emit(V, lambda e, h=h, par=par: e.tensor_scalar(
            out=diagp(par, h), in0=eye_bf[:], scalar1=st(RCP[h]), scalar2=None,
            op0=ALU.mult), inc=[(vse, 1)] if last else None)

    def emit_attn_part2(b, t, nxt, par):
        emit_nmx(t, 0)
        emit_exp(b, t, 0, par)
        emit_cmb(t, 1)
        emit_head_full(b, t, 2)
        emit_nmx(t, 1)
        emit_exp(b, t, 1, par)
        emit_cmb(t, 2)
        emit_head_full(b, t, 3)
        emit_nmx(t, 2)
        emit_exp(b, t, 2, par)
        emit_cmb(t, 3)
        spacer()
        emit_nmx(t, 3)
        emit_exp(b, t, 3, par)
        emit_cast(b, t, 0, par)
        emit_cast(b, t, 1, par)
        if nxt is not None:
            emit_head_full(nxt[0], nxt[1], 0)
        emit_cast(b, t, 2, par)
        emit_cast(b, t, 3, par)
        if nxt is not None:
            emit_head_full(nxt[0], nxt[1], 1)
        for h in range(4):
            emit_add(h)
        for h in range(4):
            emit_recip(h)
        for h in range(4):
            emit_diag(h, par, last=(h == 3))
        if nxt is not None:
            emit_cmb(nxt[1], 0)
        return vse.n

    seg_tse = {}

    def emit_pv(b, t, v_diag, par):
        nkt = t + 1
        r.wait(TE, vse, v_diag)
        if isinstance(pd_free[0], tuple):
            r.wait(TE, pd_free[0][0], pd_free[0][1])
        pt_ready = {}
        for kt in range(nkt + 1):
            if kt < nkt:
                bank = kt % 2
                sl = kt % 8
                if bank in ptp_free:
                    semx, vx = ptp_free[bank]
                    r.wait(TE, semx, vx)
                for h in range(4):
                    r.emit(TE, lambda e, h=h, kt=kt, bpar=par, b=b, bank=kt % 2: e.matmul(
                        ptp[bank][:, h * 128:(h + 1) * 128],
                        prbsl(bpar, b, h, kt * 128, kt * 128 + 128),
                        diagp(bpar, h), start=True, stop=True),
                        inc=[(tse, 1)] if h == 3 else None)
                t_pt = tse.n
                if kt % 3 != 2:
                    r.wait(SC, tse, t_pt)
                    r.emit(SC, lambda e, sl=sl, bank=bank: e.copy(
                        A(R_PTSEG, sl * 512, (sl + 1) * 512), ptp[bank][:]),
                        inc=[(sse, 1)])
                    ptp_free[bank] = (sse, sse.n)
                else:
                    r.wait(V, tse, t_pt)
                    r.emit(V, lambda e, sl=sl, bank=bank: e.tensor_copy(
                        A(R_PTSEG, sl * 512, (sl + 1) * 512), ptp[bank][:]),
                        inc=[(vse, 1)])
                    ptp_free[bank] = (vse, vse.n)
                pt_ready[kt] = ptp_free[bank]
            if kt >= 1:
                pk = kt - 1
                semx, vx = pt_ready[pk]
                r.wait(TE, semx, vx)
                r.emit(TE, lambda e, b=b, pk=pk, nkt=nkt: e.matmul(
                    PDp[:],
                    A(R_VSB, b * S + pk * 128, b * S + pk * 128 + 128),
                    A(R_PTSEG, (pk % 8) * 512, (pk % 8) * 512 + 512),
                    start=(pk == 0), stop=(pk == nkt - 1)),
                    inc=[(tse, 1)] if pk == nkt - 1 else None)
        pt_last[0] = tse.n
        pt_par[par] = tse.n
        r.wait(V, tse, tse.n)
        if t % 8 == 0:
            r.wait(V, yd, yd_stage[0])
        for h in range(4):
            r.emit(V, lambda e, h=h, t=t: e.tensor_copy(
                A(ASTG[h], (t % 8) * 128, (t % 8) * 128 + 128),
                PDp[:, h * 128:(h + 1) * 128]),
                inc=[(vse, 1)] if h == 3 else None)
        pd_free[0] = (vse, vse.n)
        if t % 8 == 3:
            shard = b * 4 + t // 8
            r.wait(SY, vse, vse.n)
            for h in range(4):
                r.emit(SY, lambda e, h=h, shard=shard: e.dma_start(
                    out=a2ix.ap()[shard, h * 128:(h + 1) * 128, :],
                    in_=A(ASTG[h], 0, 512)), inc=[(yd, 16)])
            yd_stage[0] = yd.n
            yd_stx[0] = yd.n
        if t % 8 == 7:
            shard = b * 4 + t // 8
            r.wait(SY, vse, vse.n)
            for h in range(4):
                r.emit(SY, lambda e, h=h, shard=shard: e.dma_start(
                    out=a2iy.ap()[shard, h * 128:(h + 1) * 128, :],
                    in_=A(ASTG[h], 512, 1024)), inc=[(yd, 16)])
            yd_stage[0] = yd.n

    prev = None
    emit_head01(pairs[0][0], pairs[0][1])
    for i, (b, t) in enumerate(pairs):
        if prev is not None:
            emit_pv(prev[0], prev[1], prev[2], prev[3])
        if i == 61:
            r.wait(G, yd, yd_stx[0])
            r.emit(G, lambda e: e.collective_compute(
                "AllToAll", ALU.bypass, replica_groups=[list(range(NCORES))],
                ins=[a2ix.ap()], outs=[a2ox.ap()]), inc=[(cse, 1)])
            cse_ax[0] = cse.n
        nxt = pairs[i + 1] if i + 1 < len(pairs) else None
        vd = emit_attn_part2(b, t, nxt, i % 2)
        prev = (b, t, vd, i % 2)
    emit_pv(prev[0], prev[1], prev[2], prev[3])
    r.wait(V, tse, tse.n)
    attn_v_end = vse.n
    attn_s_end = sse.n

    yd_a2 = yd.n
    r.wait(G, yd, yd_a2)
    r.emit(G, lambda e: e.collective_compute(
        "AllToAll", ALU.bypass, replica_groups=[list(range(NCORES))],
        ins=[a2iy.ap()], outs=[a2oy.ap()]), inc=[(cse, 1)])
    cse_a2 = cse.n

    r.wait(SY, cse, cse_ax[0])
    for hcg in range(32):
        rr, lr = hcg // 4, hcg % 4
        r.emit(SY, lambda e, hcg=hcg, rr=rr, lr=lr: e.dma_start(
            out=A(R_ATM, hcg * 1024, hcg * 1024 + 512),
            in_=a2ox.ap()[rr, lr * 128:(lr + 1) * 128, :]), inc=[(yd, 16)])
    yd_atm_x = yd.n
    r.wait(SY, cse, cse_a2)
    for hcg in range(32):
        rr, lr = hcg // 4, hcg % 4
        r.emit(SY, lambda e, hcg=hcg, rr=rr, lr=lr: e.dma_start(
            out=A(R_ATM, hcg * 1024 + 512, hcg * 1024 + 1024),
            in_=a2oy.ap()[rr, lr * 128:(lr + 1) * 128, :]), inc=[(yd, 16)])
    yd_atm = yd.n

    r.wait(G, cse, cse_a2)

    F_P1, F_P2, F_PJ = 2048, 2176, 2304
    OPS2 = [[PDp[:], PEp[:], PFp[:], PGp[:]],
            [PAp[:, 0:512], PAp[:, 512:1024], PBp[:, 0:512], PBp[:, 512:1024]]]
    NK = 16
    wo_use = {}
    v_wo = {}
    oq_guard = {}
    v_evac_p = {}
    xr_guard = {}
    first_mm = True
    for nk in range(NK):
        for gq in range(8):
            bo = (gq % 2) * 2048
            gk = nk * 8 + gq
            if gk >= 2:
                r.wait(G, vse, oq_guard[gk - 2])
            r.emit(G, lambda e, nk=nk, gq=gq, bo=bo: e.dma_start(
                out=A(R_OQS, bo, bo + 1024).rearrange("p (g c) -> p g c", g=4),
                in_=oc[gq * 512:(gq + 1) * 512, nk * 256:(nk + 1) * 256].rearrange(
                    "(g p) c -> p g c", p=128)), inc=[(gd, 16)])
            r.emit(G, lambda e, nk=nk, gq=gq, bo=bo: e.dma_start(
                out=A(R_OQS, bo + 1024, bo + 2048).rearrange("p (g c) -> p g c", g=4),
                in_=osr[gq * 512:(gq + 1) * 512, nk * 256:(nk + 1) * 256].rearrange(
                    "(g p) c -> p g c", p=128)), inc=[(gd, 16)])
            r.wait(V, gd, gd.n)
            if nk >= 1:
                r.wait(V, tse, wo_use[gq])
            r.emit(V, lambda e, gq=gq, bo=bo: e.scalar_tensor_tensor(
                out=A(R_WO, gq * 1024, (gq + 1) * 1024), in0=A(R_OQS, bo, bo + 1024),
                scalar=-8.0, in1=A(R_OQS, bo + 1024, bo + 2048),
                op0=ALU.add, op1=ALU.mult), inc=[(vse, 1)])
            for hcg in range(gq * 4, gq * 4 + 4):
                v_wo[(nk, hcg)] = vse.n
            oq_guard[gk] = vse.n
        for p in range(2):
            xb = (p % 2) * 1024
            grp = nk * 2 + p
            if grp < 2:
                r.wait(SY, vse, attn_v_end)
                r.wait(SY, sse, attn_s_end)
            r.wait(SY, vse, xr_guard.get(grp - 2, 0))
            for tt in range(4):
                gtt = p * 4 + tt
                r.emit(SY, lambda e, gtt=gtt, nk=nk, tt=tt, xb=xb: e.dma_start(
                    out=FA(XRS + xb + tt * 256, XRS + xb + (tt + 1) * 256),
                    in_=x_own[gtt * 128:(gtt + 1) * 128, nk * 256:(nk + 1) * 256]),
                    inc=[(yd, 16)])
            yd_x = yd.n
            if first_mm:
                r.wait(TE, yd, yd_atm_x)
                first_mm = False
            if grp == 1:
                r.wait(TE, yd, yd_atm)
            par = grp % 2
            r.wait(TE, vse, v_evac_p.get(grp - 2, 0))
            for hcg in range(32):
                r.wait(TE, vse, v_wo[(nk, hcg)])
                for tt in range(4):
                    r.emit(TE, lambda e, hcg=hcg, p=p, tt=tt, par=par: e.matmul(
                        OPS2[par][tt][:, 0:256],
                        A(R_ATM, hcg * 1024 + p * 512 + tt * 128,
                          hcg * 1024 + p * 512 + tt * 128 + 128),
                        A(R_WO, hcg * 256, (hcg + 1) * 256),
                        start=(hcg == 0), stop=(hcg == 31)),
                        inc=[(tse, 1)] if tt == 3 else None)
                wo_use[hcg // 4] = tse.n
            r.wait(V, tse, tse.n)
            r.wait(V, yd, yd_x)
            for tt in range(4):
                gtt = p * 4 + tt
                r.emit(V, lambda e, gtt=gtt, nk=nk, tt=tt, xb=xb, par=par: e.scalar_tensor_tensor(
                    out=A(R_HPRE, gtt * 4096 + nk * 256, gtt * 4096 + (nk + 1) * 256),
                    in0=OPS2[par][tt][:, 0:256], scalar=0.25,
                    in1=FA(XRS + xb + tt * 256, XRS + xb + (tt + 1) * 256),
                    op0=ALU.mult, op1=ALU.add,
                    accum_out=FA(F_P1 + gtt * 16 + nk, F_P1 + gtt * 16 + nk + 1)),
                    inc=[(vse, 1)] if tt == 3 else None)
            for tt in range(4):
                gtt = p * 4 + tt
                r.emit(V, lambda e, gtt=gtt, nk=nk: e.scalar_tensor_tensor(
                    out=FA(F_PJ, F_PJ + 256),
                    in0=A(R_HPRE, gtt * 4096 + nk * 256, gtt * 4096 + (nk + 1) * 256),
                    scalar=1.0,
                    in1=A(R_HPRE, gtt * 4096 + nk * 256, gtt * 4096 + (nk + 1) * 256),
                    op0=ALU.mult, op1=ALU.mult,
                    accum_out=FA(F_P2 + gtt * 16 + nk, F_P2 + gtt * 16 + nk + 1)))
            v_evac_p[grp] = vse.n
            xr_guard[grp] = vse.n
    v_eall = vse.n


    def emit_rsqrt_cols(vpe_c, rstd_c, nt1_c, nt2_c, nt3_c, w=1):
        vpe, rstd = st(vpe_c, w), st(rstd_c, w)
        n1, n2, n3 = st(nt1_c, w), st(nt2_c, w), st(nt3_c, w)
        r.emit(V, lambda e: e.tensor_scalar(
            out=n1.bitcast(I32), in0=vpe.bitcast(I32), scalar1=1,
            scalar2=None, op0=ALU.arith_shift_right))
        spacer()
        r.emit(V, lambda e: e.tensor_scalar(
            out=rstd.bitcast(I32), in0=n1.bitcast(I32), scalar1=-1,
            scalar2=0x5F3759DF, op0=ALU.mult, op1=ALU.add))
        spacer()
        for _ in range(3):
            r.emit(V, lambda e: e.tensor_tensor(out=n1, in0=rstd, in1=rstd, op=ALU.mult))
            spacer()
            r.emit(V, lambda e: e.tensor_tensor(out=n2, in0=vpe, in1=n1, op=ALU.mult))
            spacer()
            r.emit(V, lambda e: e.tensor_scalar(
                out=n3, in0=n2, scalar1=-0.5, scalar2=1.5, op0=ALU.mult, op1=ALU.add))
            spacer()
            r.emit(V, lambda e: e.tensor_tensor(out=rstd, in0=rstd, in1=n3, op=ALU.mult))
            spacer()

    yd_out = [0]
    r.wait(V, vse, v_eall)

    def hp(a, b, tt):
        return A(R_HPRE, tt * 4096 + a, tt * 4096 + b)

    def hh(a, b, tt):
        return A(R_H, tt * 4096 + a, tt * 4096 + b)

    for tt in range(8):
        r.emit(V, lambda e, tt=tt: e.reduce_sum(
            st(B_S1 + tt), FA(F_P1 + tt * 16, F_P1 + tt * 16 + 16),
            axis=mybir.AxisListType.X))
        r.emit(V, lambda e, tt=tt: e.reduce_sum(
            st(B_S2 + tt), FA(F_P2 + tt * 16, F_P2 + tt * 16 + 16),
            axis=mybir.AxisListType.X))
    spacer(2)
    INV_N = 1.0 / 4096.0
    r.emit(V, lambda e: e.tensor_scalar(
        out=st(B_MEAN, 8), in0=st(B_S1, 8), scalar1=INV_N, scalar2=None, op0=ALU.mult))
    r.emit(V, lambda e: e.tensor_scalar(
        out=st(B_EVAR, 8), in0=st(B_S2, 8), scalar1=INV_N, scalar2=float(EPS),
        op0=ALU.mult, op1=ALU.add))
    r.emit(V, lambda e: e.tensor_scalar(
        out=st(B_NMEAN, 8), in0=st(B_MEAN, 8), scalar1=-1.0, scalar2=None, op0=ALU.mult))
    r.emit(V, lambda e: e.tensor_tensor(
        out=st(B_MSQ, 8), in0=st(B_MEAN, 8), in1=st(B_MEAN, 8), op=ALU.mult))
    spacer()
    r.emit(V, lambda e: e.tensor_tensor(
        out=st(B_VPE, 8), in0=st(B_EVAR, 8), in1=st(B_MSQ, 8), op=ALU.subtract))
    spacer()
    emit_rsqrt_cols(B_VPE, B_RSTD, B_S1, B_S2, B_MSQ, w=8)
    r.emit(V, lambda e: e.tensor_tensor(
        out=st(B_NB, 8), in0=st(B_NMEAN, 8), in1=st(B_RSTD, 8), op=ALU.mult))
    spacer(2)
    r.emit(V, lambda e: e.nop(), inc=[(vse, 1)])

    LG = [PFp, PGp]
    v_lgread = {}
    t_lg_map = {}
    s_h_map = {}
    exp_sse_f = {}

    def emit_router(tt):
        lgb = LG[tt % 2]
        r.emit(V, lambda e, tt=tt: e.tensor_scalar(
            out=hh(0, 4096, tt), in0=hp(0, 4096, tt), scalar1=st(B_RSTD + tt),
            scalar2=st(B_NB + tt), op0=ALU.mult, op1=ALU.add), inc=[(vse, 1)])
        v_h = vse.n
        r.wait(SC, vse, v_h)
        r.emit(SC, lambda e: e.nop(), inc=[(sse, 1)])
        s_h_map[tt] = sse.n
        if tt >= 2:
            r.wait(TE, vse, v_lgread[tt - 2])
        ht_ready = {}
        for hc in range(34):
            if hc >= 2:
                mm = hc - 2
                slot = mm % 2
                r.wait(TE, sse, ht_ready[mm])
                r.emit(TE, lambda e, mm=mm, slot=slot, lgb=lgb: e.matmul(
                    lgb[:, 0:E], A(R_HT, slot * 128, (slot + 1) * 128),
                    wrt_sb[:, mm * E:(mm + 1) * E],
                    start=(mm == 0), stop=False), inc=[(tse, 1)])
            if hc < 32:
                slot = hc % 2
                r.wait(TE, sse, s_h_map[tt])
                if hc >= 2:
                    r.wait(TE, sse, ht_ready[hc - 2])
                pp = PEp if slot == 0 else PDp
                r.emit(TE, lambda e, hc=hc, pp=pp, tt=tt: e.transpose(
                    pp[:, 0:64].bitcast(BF16),
                    hh(hc * 128, (hc + 1) * 128, tt), eye_bf[:]), inc=[(tse, 1)])
                t_tr = tse.n
                r.wait(SC, tse, t_tr)
                r.emit(SC, lambda e, slot=slot, pp=pp: e.copy(
                    A(R_HT, slot * 128, (slot + 1) * 128),
                    pp[:, 0:64].bitcast(BF16)), inc=[(sse, 1)])
                ht_ready[hc] = sse.n
        r.emit(TE, lambda e, lgb=lgb: e.matmul(
            lgb[:, 0:E], one_bf[:], rb_bf[:], start=False, stop=True), inc=[(tse, 1)])
        t_lg_map[tt] = tse.n

    def emit_soft_head(tt):
        lgb = LG[tt % 2]
        r.wait(V, tse, t_lg_map[tt])
        if tt >= 1:
            r.wait(V, sse, exp_sse_f[tt - 1])
        r.emit(V, lambda e, lgb=lgb: e.tensor_scalar(
            out=lg_sb[:], in0=lgb[:, 0:E], scalar1=1.0, scalar2=FMIN,
            op0=ALU.mult, op1=ALU.max, accum_out=st(C_M64)), inc=[(vse, 1)])
        v_lgread[tt] = vse.n
        spacer()
        r.emit(V, lambda e: e.tensor_scalar(
            out=st(C_NM64), in0=st(C_M64), scalar1=-1.0, scalar2=None,
            op0=ALU.mult), inc=[(vse, 1)])
        r.wait(SC, vse, vse.n)
        r.emit(SC, lambda e: e.activation(
            ex_sb[:], lg_sb[:], AF.Exp, bias=st(C_NM64), scale=1.0), inc=[(sse, 1)])
        exp_sse_f[tt] = sse.n

    def emit_soft_tail(tt):
        r.wait(V, sse, exp_sse_f[tt])
        r.emit(V, lambda e: e.tensor_scalar(
            out=cur_sb[:, 0:E], in0=ex_sb[:], scalar1=1.0, scalar2=0.0,
            op0=ALU.mult, op1=ALU.add, accum_out=st(C_SUM0)))
        r.emit(V, lambda e: e.memset(st(C_WS), 0.0))
        for j in range(8):
            ca = (j % 2) * E
            cb = ((j + 1) % 2) * E
            r.emit(V, lambda e, ca=ca: e.reduce_max(
                st(C_MJ), cur_sb[:, ca:ca + E], axis=mybir.AxisListType.X))
            spacer(2)
            r.emit(V, lambda e: e.tensor_scalar(
                out=st(C_MJC), in0=st(C_MJ), scalar1=0.0, scalar2=None,
                op0=ALU.max))
            if j < 7:
                r.emit(V, lambda e, ca=ca: e.tensor_scalar(
                    out=msk_sb[:], in0=cur_sb[:, ca:ca + E], scalar1=st(C_MJ),
                    scalar2=None, op0=ALU.is_ge))
                r.emit(V, lambda e: e.tensor_tensor(
                    out=st(C_WS), in0=st(C_WS), in1=st(C_MJC), op=ALU.add))
                r.emit(V, lambda e, ca=ca, cb=cb: e.scalar_tensor_tensor(
                    out=cur_sb[:, cb:cb + E], in0=msk_sb[:], scalar=-1e30,
                    in1=cur_sb[:, ca:ca + E], op0=ALU.mult, op1=ALU.add))
                spacer()
            else:
                spacer()
                r.emit(V, lambda e: e.tensor_tensor(
                    out=st(C_WS), in0=st(C_WS), in1=st(C_MJC), op=ALU.add))
        r.emit(V, lambda e: e.tensor_scalar(
            out=msk_sb[:], in0=ex_sb[:], scalar1=st(C_MJ), scalar2=None,
            op0=ALU.is_ge))
        r.emit(V, lambda e: e.tensor_tensor(
            out=st(C_COEF), in0=st(C_SUM0), in1=st(C_WS), op=ALU.add))
        r.emit(V, lambda e: e.tensor_tensor(
            out=A(R_WDS), in0=ex_sb[:], in1=msk_sb[:], op=ALU.mult),
            inc=[(vse, 1)])
        v_wd = vse.n
        r.wait(TE, vse, v_wd)
        r.emit(TE, lambda e: e.transpose(
            PAp[0:E, 0:64].bitcast(BF16), A(R_WDS), eye_bf[:]), inc=[(tse, 1)])
        r.wait(SC, tse, tse.n)
        r.emit(SC, lambda e: e.copy(
            ABF[0:E, 75072:75072 + 128], PAp[0:E, 0:64].bitcast(BF16)),
            inc=[(sse, 1)])
        s_wdt = sse.n
        r.wait(TE, sse, s_wdt)
        v_gun = {}
        for nb4 in range(4):
            ps = PAp if nb4 % 2 == 0 else PBp
            if nb4 >= 2:
                r.wait(TE, vse, v_gun[nb4 - 2])
            for half in range(2):
                r.emit(TE, lambda e, ps=ps, nb4=nb4, half=half: e.matmul(
                    ps[:, half * 512:(half + 1) * 512], ABF[0:E, 75072:75072 + 128],
                    eb_sb[:, nb4 * 1024 + half * 512:nb4 * 1024 + (half + 1) * 512],
                    start=True, stop=True), inc=[(tse, 1)] if half == 1 else None)
            r.wait(V, tse, tse.n)
            r.emit(V, lambda e, ps=ps, nb4=nb4, tt=tt: e.scalar_tensor_tensor(
                out=FA(GUN + nb4 * 1024, GUN + (nb4 + 1) * 1024),
                in0=hh(nb4 * 1024, (nb4 + 1) * 1024, tt), scalar=st(C_COEF),
                in1=ps[:], op0=ALU.mult, op1=ALU.add, accum_out=st(C_P1 + nb4)))
            r.emit(V, lambda e, nb4=nb4: e.scalar_tensor_tensor(
                out=FA(OUTB, OUTB + 1024),
                in0=FA(GUN + nb4 * 1024, GUN + (nb4 + 1) * 1024), scalar=1.0,
                in1=FA(GUN + nb4 * 1024, GUN + (nb4 + 1) * 1024),
                op0=ALU.mult, op1=ALU.mult, accum_out=st(C_P2 + nb4)),
                inc=[(vse, 1)])
            v_gun[nb4] = vse.n
        r.emit(V, lambda e: e.reduce_sum(
            st(C_SUM), st(C_P1, 4), axis=mybir.AxisListType.X))
        r.emit(V, lambda e: e.reduce_sum(
            st(C_SSQ), st(C_P2, 4), axis=mybir.AxisListType.X))
        spacer(2)
        r.emit(V, lambda e: e.tensor_scalar(
            out=st(C_MEAN), in0=st(C_SUM), scalar1=INV_N, scalar2=None, op0=ALU.mult))
        r.emit(V, lambda e: e.tensor_scalar(
            out=st(C_EVAR), in0=st(C_SSQ), scalar1=INV_N, scalar2=float(EPS),
            op0=ALU.mult, op1=ALU.add))
        r.emit(V, lambda e: e.tensor_tensor(
            out=st(C_MSQ), in0=st(C_MEAN), in1=st(C_MEAN), op=ALU.mult))
        spacer()
        r.emit(V, lambda e: e.tensor_tensor(
            out=st(C_VPE), in0=st(C_EVAR), in1=st(C_MSQ), op=ALU.subtract))
        spacer()
        emit_rsqrt_cols(C_VPE, C_RSTD, C_NT1, C_NT2, C_NT3)
        r.emit(V, lambda e: e.tensor_scalar(
            out=st(C_NB), in0=st(C_MEAN), scalar1=st(C_RSTD), scalar2=-1.0,
            op0=ALU.mult, op1=ALU.mult))
        spacer()
        r.wait(V, yd, yd_out[0])
        r.emit(V, lambda e: e.tensor_scalar(
            out=FA(OUTB, OUTB + 4096), in0=FA(GUN, GUN + 4096),
            scalar1=st(C_RSTD), scalar2=st(C_NB), op0=ALU.mult, op1=ALU.add),
            inc=[(vse, 1)])
        r.wait(SY, vse, vse.n)
        r.emit(SY, lambda e, tt=tt: e.dma_start(
            out=out[tt * 128:(tt + 1) * 128, :], in_=FA(OUTB, OUTB + 4096)),
            inc=[(yd, 16)])
        yd_out[0] = yd.n

    emit_router(0)
    for tt in range(8):
        emit_soft_head(tt)
        if tt + 1 < 8:
            emit_router(tt + 1)
        emit_soft_tail(tt)

    r.wait(SY, yd, yd.n)

    with nc.Block() as blk:
        @blk.tensor
        def _(e):
            r.play("tensor", e)

        @blk.vector
        def _(e):
            r.play("vector", e)

        @blk.scalar
        def _(e):
            r.play("scalar", e)

        @blk.gpsimd
        def _(e):
            r.play("gpsimd", e)

        @blk.sync
        def _(e):
            r.play("sync", e)

    ctx.close()
    return nc


_NC_CACHE = [None]


def kernel(L0_attn_x, L0_attn_q_codes, L0_attn_k_codes, L0_attn_v_codes,
           L0_attn_o_codes, L0_attn_q_scales, L0_attn_k_scales,
           L0_attn_v_scales, L0_attn_o_scales, L0_ffn_router_weight,
           L0_ffn_router_bias, L0_ffn_expert_bias):
    xf = np.ascontiguousarray(np.asarray(L0_attn_x, np.float32).reshape(T, DM))
    qcod = np.ascontiguousarray(np.asarray(L0_attn_q_codes, np.int32))
    kcod = np.ascontiguousarray(np.asarray(L0_attn_k_codes, np.int32))
    vcod = np.ascontiguousarray(np.asarray(L0_attn_v_codes, np.int32))
    ocod = np.ascontiguousarray(np.asarray(L0_attn_o_codes, np.int32))

    def rep(s):
        return np.ascontiguousarray(np.repeat(np.asarray(s, np.float32), 128, axis=0))

    qsr, ksr, vsr, osrr = (rep(s) for s in (
        L0_attn_q_scales, L0_attn_k_scales, L0_attn_v_scales, L0_attn_o_scales))
    rww = np.ascontiguousarray(np.asarray(L0_ffn_router_weight, np.float32))
    rbb = np.ascontiguousarray(np.asarray(L0_ffn_router_bias, np.float32).reshape(1, E))
    ebb = np.ascontiguousarray(np.asarray(L0_ffn_expert_bias, np.float32))

    half = 64
    inv = (1.0 / (10000.0 ** (np.arange(half, dtype=np.float32) / np.float32(half)))).astype(np.float32)
    pos = np.arange(S, dtype=np.float32)
    ang = inv[:, None] * pos[None, :]
    ang2 = np.concatenate([ang, ang], axis=0).astype(np.float32)
    cosT = np.cos(ang2).astype(np.float32)
    sinT = np.sin(ang2).astype(np.float32)
    sinN = np.concatenate([-sinT[0:64], sinT[64:128]], 0)
    eye = np.eye(128, dtype=np.float32)
    rotMT = np.zeros((128, 128), np.float32)
    for d in range(64):
        rotMT[d + 64, d] = -1.0
        rotMT[d, d + 64] = 1.0
    tri = np.triu(np.full((128, 128), np.float32(-1.0e30), np.float32), k=1)
    ones1 = np.ones((1, 128), np.float32)

    xtf = np.ascontiguousarray(xf.T)
    in_maps = []
    for i in range(NCORES):
        in_maps.append(dict(
            x_own=np.ascontiguousarray(xf[i * TPC:(i + 1) * TPC]),
            xt=xtf,
            qc=np.ascontiguousarray(qcod[:, i * QSL:(i + 1) * QSL]),
            kc=np.ascontiguousarray(kcod[:, i * D:(i + 1) * D]),
            vc=np.ascontiguousarray(vcod[:, i * D:(i + 1) * D]),
            oc=ocod,
            qs=np.ascontiguousarray(qsr[:, i * QSL:(i + 1) * QSL]),
            ks=np.ascontiguousarray(ksr[:, i * D:(i + 1) * D]),
            vs=np.ascontiguousarray(vsr[:, i * D:(i + 1) * D]),
            osr=osrr,
            rw=rww, rb=rbb, ebi=ebb,
            cosc=cosT, sinc=sinN, eyec=eye, tric=tri, onec=ones1,
        ))

    if _NC_CACHE[0] is None:
        with allow_ici_all_to_all():
            _NC_CACHE[0] = build_nc()
    res = run_bass_kernel_spmd(_NC_CACHE[0], in_maps, core_ids=list(range(NCORES)))
    full = np.concatenate([res.results[i]["out"] for i in range(NCORES)], axis=0)
    return full.reshape(B, S, DM).astype(np.float32)


# version 24
# speedup vs baseline: 1.2179x; 1.0449x over previous
import numpy as np
from contextlib import ExitStack
import concourse.bass as bass
import concourse.mybir as mybir
from concourse.bass_utils import run_bass_kernel_spmd
from concourse.aot_env import allow_ici_all_to_all

F32 = mybir.dt.float32
BF16 = mybir.dt.bfloat16
I32 = mybir.dt.int32
AF = mybir.ActivationFunctionType
ALU = mybir.AluOpType

NCORES = 8
B, S, DM = 2, 4096, 4096
T = B * S
TPC = T // NCORES
D = 128
QSL = 512
E = 64
SCALE = float((1.0 / np.sqrt(np.float32(128.0))) / 4.0)
EPS = 1e-5
FMIN = -3.0e38

ENGS = ("tensor", "vector", "scalar", "gpsimd", "sync")
TE, V, SC, G, SY = ENGS


class Sem:
    def __init__(self, handle):
        self.h = handle
        self.n = 0


class Rec:
    def __init__(self):
        self.ops = {e: [] for e in ENGS}

    def emit(self, eng, fn, inc=None):
        incs = []
        if inc:
            for s, by in inc:
                s.n += by
                incs.append((s, by))
        self.ops[eng].append(("op", fn, incs))

    def wait(self, eng, sem, val=None):
        v = sem.n if val is None else val
        if v <= 0:
            return
        lst = self.ops[eng]
        if lst and lst[-1][0] == "wait" and lst[-1][1] is sem:
            if v > lst[-1][2]:
                lst[-1] = ("wait", sem, v)
            return
        lst.append(("wait", sem, v))

    def play(self, eng_name, e):
        for kind, a, b in self.ops[eng_name]:
            if kind == "wait":
                e.wait_ge(a.h, b)
            else:
                ins = a(e)
                for s, by in b:
                    ins.then_inc(s.h, by)


def build_nc():
    nc = bass.Bass("TRN2", target_bir_lowering=False, debug=False)

    x_own = nc.dram_tensor("x_own", [TPC, DM], F32, kind="ExternalInput").ap()
    xt = nc.dram_tensor("xt", [DM, T], F32, kind="ExternalInput").ap()
    qc = nc.dram_tensor("qc", [DM, QSL], I32, kind="ExternalInput").ap()
    kc = nc.dram_tensor("kc", [DM, D], I32, kind="ExternalInput").ap()
    vc = nc.dram_tensor("vc", [DM, D], I32, kind="ExternalInput").ap()
    oc = nc.dram_tensor("oc", [DM, DM], I32, kind="ExternalInput").ap()
    qs = nc.dram_tensor("qs", [DM, QSL], F32, kind="ExternalInput").ap()
    ks = nc.dram_tensor("ks", [DM, D], F32, kind="ExternalInput").ap()
    vs = nc.dram_tensor("vs", [DM, D], F32, kind="ExternalInput").ap()
    osr = nc.dram_tensor("osr", [DM, DM], F32, kind="ExternalInput").ap()
    rw = nc.dram_tensor("rw", [E, DM], F32, kind="ExternalInput").ap()
    rb = nc.dram_tensor("rb", [1, E], F32, kind="ExternalInput").ap()
    ebi = nc.dram_tensor("ebi", [E, DM], F32, kind="ExternalInput").ap()
    cosc = nc.dram_tensor("cosc", [128, S], F32, kind="ExternalInput").ap()
    sinc = nc.dram_tensor("sinc", [128, S], F32, kind="ExternalInput").ap()
    eyec = nc.dram_tensor("eyec", [128, 128], F32, kind="ExternalInput").ap()
    tric = nc.dram_tensor("tric", [128, 128], F32, kind="ExternalInput").ap()
    onec = nc.dram_tensor("onec", [1, 128], F32, kind="ExternalInput").ap()
    out = nc.dram_tensor("out", [TPC, DM], F32, kind="ExternalOutput").ap()

    a2ix = nc.dram_tensor("a2ix", [NCORES, QSL, TPC // 2], BF16)
    a2iy = nc.dram_tensor("a2iy", [NCORES, QSL, TPC // 2], BF16)
    a2ox = nc.dram_tensor("a2ox", [NCORES, QSL, TPC // 2], BF16)
    a2oy = nc.dram_tensor("a2oy", [NCORES, QSL, TPC // 2], BF16)

    ctx = ExitStack()
    sb = lambda name, shape, dt: ctx.enter_context(nc.sbuf_tensor(name, shape, dt))

    ABF = sb("abf", [128, 77824], BF16)
    AFR = sb("afr", [128, 8192], F32)
    eye_bf = sb("eye_bf", [128, 128], BF16)
    one_bf = sb("one_bf", [1, 128], BF16)
    rb_bf = sb("rb_bf", [1, E], BF16)
    tri_f = sb("tri_f", [128, 128], F32)
    wrt_sb = sb("wrt_sb", [128, 32 * E], BF16)
    eb_sb = sb("eb_sb", [E, DM], BF16)
    rw_bf = sb("rw_bf", [E, DM], BF16)
    stat = sb("stat", [128, 104], F32)
    lg_sb = sb("lg_sb", [128, E], F32)
    ex_sb = sb("ex_sb", [128, E], F32)
    cur_sb = sb("cur_sb", [128, 2 * E], F32)
    msk_sb = sb("msk_sb", [128, E], F32)

    def A(role, a=0, b=None):
        off, w = role
        if b is None:
            b = w
        assert 0 <= a <= b <= w, (role, a, b)
        return ABF[:, off + a: off + b]

    def FA(a, b):
        return AFR[:, a:b]

    R_QROPE = (0, 4 * T)
    R_KROPE = (32768, T)
    R_VSB = (40960, T)
    R_WQ = (49152, 32 * 512)
    R_WK = (65536, 32 * 128)
    R_WV = (69632, 32 * 128)
    R_DQS = (73728, 4096)
    R_XTS = (73728, 2 * 2048)
    R_XTOWN = (0, 32 * 1024)
    R_XBF = (32768, 2 * 4096)
    R_KVST = (40960, 8192)
    R_COS = (49152, 1024)
    R_SIN = (50176, 1024)
    R_RT1 = (51200, 1024)
    R_RT2 = (52224, 1024)
    PRB = [(49152 + h * 4096, 4096) for h in range(4)]
    R_PTSEG = (65536, 4096)
    ASTG = [(69632 + h * 1024, 1024) for h in range(4)]
    DIAG = [(73728 + h * 128, 128) for h in range(4)]
    R_JUNK = (74240, 2048)
    DIAG2 = [(76288 + h * 128, 128) for h in range(4)]
    R_ATM = (0, 32 * 1024)
    R_WO = (32768, 32 * 256)
    R_HPRE = (40960, 8 * 4096)
    R_OQS = (73728, 4096)
    R_H = (0, 8 * 4096)
    R_HT = (74752, 256)
    R_WDS = (75008, 64)
    R_WDT = (75072, 128)

    SRA, SRB = 0, 4096
    XRS = 0
    GUN = 0
    OUTB = 4096

    MXS = [[h * 4 + s for s in range(4)] for h in range(4)]
    M01 = [16 + h for h in range(4)]
    M23 = [20 + h for h in range(4)]
    MXC = [24 + h for h in range(4)]
    NMX = [28 + h for h in range(4)]
    SMS = [32 + h for h in range(4)]
    RCP = [36 + h for h in range(4)]
    C_SUM, C_SSQ, C_MEAN, C_EVAR, C_MSQ, C_VPE, C_RSTD, C_NB = 0, 1, 2, 3, 4, 5, 6, 7
    C_NT1, C_NT2, C_NT3 = 8, 9, 10
    C_M64, C_NM64, C_SUM0, C_WS, C_MJ, C_COEF = 11, 12, 13, 14, 15, 16
    C_P1, C_P2 = 96, 100
    C_SP1, C_SP2, C_MJC = 17, 18, 19
    B_S1, B_S2, B_MEAN, B_NMEAN, B_EVAR, B_MSQ, B_VPE, B_RSTD, B_NB = (
        40, 48, 56, 64, 72, 80, 88, 20, 28)
    C_BN = 32

    def st(c, w=1):
        return stat[:, c:c + w]

    PAp = ctx.enter_context(nc.psum_tensor("pa", [128, 1024], F32))
    PBp = ctx.enter_context(nc.psum_tensor("pb", [128, 1024], F32))
    PDp = ctx.enter_context(nc.psum_tensor("pd", [128, 512], F32))
    PEp = ctx.enter_context(nc.psum_tensor("pe", [128, 512], F32))
    PFp = ctx.enter_context(nc.psum_tensor("pf", [128, 512], F32))
    PGp = ctx.enter_context(nc.psum_tensor("pg", [128, 512], F32))

    sems = {k: Sem(ctx.enter_context(nc.semaphore(k)))
            for k in ("gd", "yd", "ts", "vs", "ss", "cs")}
    gd, yd, tse, vse, sse, cse = (sems[k] for k in ("gd", "yd", "ts", "vs", "ss", "cs"))

    r = Rec()

    def spacer(n=1):
        for _ in range(n):
            r.emit(V, lambda e: e.tensor_scalar(
                out=st(C_SP1), in0=st(C_SP2), scalar1=1.0, scalar2=None,
                op0=ALU.mult))

    r.emit(SC, lambda e: e.preload_act_table(AF.Exp))
    r.emit(G, lambda e: e.dma_start(out=eye_bf[:], in_=eyec), inc=[(gd, 16)])
    r.emit(G, lambda e: e.dma_start(out=one_bf[:], in_=onec), inc=[(gd, 16)])
    r.emit(G, lambda e: e.dma_start(out=rb_bf[:], in_=rb), inc=[(gd, 16)])
    r.emit(G, lambda e: e.dma_start(out=tri_f[:], in_=tric), inc=[(gd, 16)])
    r.emit(G, lambda e: e.dma_start(out=eb_sb[:], in_=ebi), inc=[(gd, 16)])
    r.emit(G, lambda e: e.dma_start(out=rw_bf[:], in_=rw), inc=[(gd, 16)])
    gd_consts = gd.n

    r.wait(TE, gd, gd_consts)

    vse_wq = {}
    for gq in range(8):
        if gq >= 1:
            r.wait(G, vse, vse_wq[gq - 1])
        r.emit(G, lambda e, gq=gq: e.dma_start(
            out=A(R_DQS, 0, 2048).rearrange("p (g c) -> p g c", g=4),
            in_=qc[gq * 512:(gq + 1) * 512, :].rearrange("(g p) c -> p g c", p=128)),
            inc=[(gd, 16)])
        r.emit(G, lambda e, gq=gq: e.dma_start(
            out=A(R_DQS, 2048, 4096).rearrange("p (g c) -> p g c", g=4),
            in_=qs[gq * 512:(gq + 1) * 512, :].rearrange("(g p) c -> p g c", p=128)),
            inc=[(gd, 16)])
        r.wait(V, gd, gd.n)
        r.emit(V, lambda e, gq=gq: e.scalar_tensor_tensor(
            out=A(R_WQ, gq * 2048, (gq + 1) * 2048), in0=A(R_DQS, 0, 2048),
            scalar=-8.0, in1=A(R_DQS, 2048, 4096), op0=ALU.add, op1=ALU.mult),
            inc=[(vse, 1)])
        vse_wq[gq] = vse.n
    r.emit(G, lambda e: e.dma_start(
        out=A(R_KVST, 0, 4096).rearrange("p (g c) -> p g c", g=32), in_=kc.rearrange("(g p) c -> p g c", p=128)),
        inc=[(gd, 16)])
    r.emit(G, lambda e: e.dma_start(
        out=A(R_KVST, 4096, 8192).rearrange("p (g c) -> p g c", g=32), in_=ks.rearrange("(g p) c -> p g c", p=128)),
        inc=[(gd, 16)])
    r.wait(V, gd, gd.n)
    r.emit(V, lambda e: e.scalar_tensor_tensor(
        out=A(R_WK), in0=A(R_KVST, 0, 4096), scalar=-8.0,
        in1=A(R_KVST, 4096, 8192), op0=ALU.add, op1=ALU.mult), inc=[(vse, 1)])
    v_wk = vse.n
    r.wait(G, vse, v_wk)
    r.emit(G, lambda e: e.dma_start(
        out=A(R_KVST, 0, 4096).rearrange("p (g c) -> p g c", g=32), in_=vc.rearrange("(g p) c -> p g c", p=128)),
        inc=[(gd, 16)])
    r.emit(G, lambda e: e.dma_start(
        out=A(R_KVST, 4096, 8192).rearrange("p (g c) -> p g c", g=32), in_=vs.rearrange("(g p) c -> p g c", p=128)),
        inc=[(gd, 16)])
    r.wait(V, gd, gd.n)
    r.emit(V, lambda e: e.scalar_tensor_tensor(
        out=A(R_WV), in0=A(R_KVST, 0, 4096), scalar=-8.0,
        in1=A(R_KVST, 4096, 8192), op0=ALU.add, op1=ALU.mult), inc=[(vse, 1)])
    vse_wall = vse.n

    sse_wrt = {}
    for hc in range(32):
        pp = PDp if hc % 2 == 0 else PEp
        if hc >= 2:
            r.wait(TE, sse, sse_wrt[hc - 2])
        r.emit(TE, lambda e, hc=hc, pp=pp: e.transpose(
            pp[:, 0:32].bitcast(BF16),
            rw_bf[:, hc * 128:(hc + 1) * 128], eye_bf[0:E, 0:E]), inc=[(tse, 1)])
        r.wait(SC, tse, tse.n)
        r.emit(SC, lambda e, hc=hc, pp=pp: e.copy(
            wrt_sb[:, hc * E:(hc + 1) * E],
            pp[:, 0:32].bitcast(BF16)),
            inc=[(sse, 1)])
        sse_wrt[hc] = sse.n

    def wsl(dt, hc):
        if dt < 4:
            return A(R_WQ, hc * 512 + dt * 128, hc * 512 + dt * 128 + 128)
        if dt == 4:
            return A(R_WK, hc * 128, hc * 128 + 128)
        return A(R_WV, hc * 128, hc * 128 + 128)

    def rawsl2(dt, tc, a, b):
        if dt < 4:
            return A(R_QROPE, dt * T + tc * 1024 + a, dt * T + tc * 1024 + b)
        if dt == 4:
            return A(R_KROPE, tc * 1024 + a, tc * 1024 + b)
        return A(R_VSB, tc * 1024 + a, tc * 1024 + b)

    def rawsl(dt, tc):
        if dt < 4:
            return A(R_QROPE, dt * T + tc * 1024, dt * T + (tc + 1) * 1024)
        if dt == 4:
            return A(R_KROPE, tc * 1024, (tc + 1) * 1024)
        return A(R_VSB, tc * 1024, (tc + 1) * 1024)

    sse_proj = {}
    tse_xuse = {}

    def ptgt(dt):
        if dt == 0: return PAp[:, 0:512]
        if dt == 1: return PAp[:, 512:1024]
        if dt == 2: return PBp[:, 0:512]
        if dt == 3: return PBp[:, 512:1024]
        if dt == 4: return PDp[:]
        return PEp[:]

    F_COS = [0, 1024]
    F_SINN = [2048, 3072]
    F_VBNC = 4096
    F_SWP3 = [4608, 5120, 5632]
    F_RT1 = 6144
    F_RT2 = 7168
    gd_cs = {}
    vse_rope = {}
    sse_vt = {}
    vt_tse = {}

    def srcp(tc, j, a, b, p0=0, p1=128):
        if j < 4:
            base = R_QROPE[0] + j * T + tc * 1024
        else:
            base = R_KROPE[0] + tc * 1024
        return ABF[p0:p1, base + a:base + b]

    def emit_vt(gv):
        if gv >= 1:
            r.wait(V, tse, vt_tse[gv - 1])
        r.wait(V, sse, sse_proj[gv * 2 + 1])
        r.emit(V, lambda e, gv=gv: e.tensor_copy(
            AFR[:, F_VBNC:F_VBNC + 512].bitcast(BF16),
            A(R_VSB, gv * 1024, (gv + 1) * 1024)), inc=[(vse, 1)])
        r.wait(TE, vse, vse.n)
        if gv >= 1:
            r.wait(TE, sse, sse_vt[gv - 1])
        for idx in range(8):
            pp = PFp if idx < 4 else PGp
            r.emit(TE, lambda e, idx=idx, pp=pp: e.transpose(
                pp[:, (idx % 4) * 64:(idx % 4) * 64 + 64].bitcast(BF16),
                AFR[:, F_VBNC + idx * 64:F_VBNC + (idx + 1) * 64].bitcast(BF16),
                eye_bf[:]),
                inc=[(tse, 1)] if idx in (3, 7) else None)
        vt_tse[gv] = tse.n
        r.wait(SC, tse, tse.n)
        r.emit(SC, lambda e, gv=gv: e.copy(
            A(R_VSB, gv * 1024, gv * 1024 + 512), PFp[:, 0:256].bitcast(BF16)))
        r.emit(SC, lambda e, gv=gv: e.copy(
            A(R_VSB, gv * 1024 + 512, gv * 1024 + 1024), PGp[:, 0:256].bitcast(BF16)),
            inc=[(sse, 1)])
        sse_vt[gv] = sse.n

    def emit_rope(tc):
        parc = tc % 2
        pos0 = (tc * 1024) % S
        if tc >= 2:
            r.wait(G, vse, vse_rope[(tc - 2) * 5 + 4])
        r.emit(G, lambda e, pos0=pos0, parc=parc: e.dma_start(
            out=FA(F_COS[parc], F_COS[parc] + 1024),
            in_=cosc[:, pos0:pos0 + 1024]), inc=[(gd, 16)])
        r.emit(G, lambda e, pos0=pos0, parc=parc: e.dma_start(
            out=FA(F_SINN[parc], F_SINN[parc] + 1024),
            in_=sinc[:, pos0:pos0 + 1024]), inc=[(gd, 16)])
        gd_cs[tc] = gd.n
        r.wait(SY, sse, sse_proj[tc * 2 + 1])
        r.wait(V, sse, sse_proj[tc * 2 + 1])
        r.wait(V, gd, gd_cs[tc])
        for j in range(5):
            idx = tc * 5 + j
            sl = F_SWP3[idx % 3]
            if idx >= 3:
                r.wait(SY, vse, vse_rope[idx - 3])
            r.emit(SY, lambda e, tc=tc, j=j, sl=sl: e.dma_start(
                out=AFR[0:64, sl:sl + 512].bitcast(BF16),
                in_=srcp(tc, j, 0, 1024, 64, 128)), inc=[(yd, 16)])
            r.emit(SY, lambda e, tc=tc, j=j, sl=sl: e.dma_start(
                out=AFR[64:128, sl:sl + 512].bitcast(BF16),
                in_=srcp(tc, j, 0, 1024, 0, 64)), inc=[(yd, 16)])
            r.wait(V, yd, yd.n)
            r.emit(V, lambda e, tc=tc, j=j, parc=parc: e.tensor_tensor(
                out=FA(F_RT1, F_RT1 + 1024), in0=srcp(tc, j, 0, 1024),
                in1=FA(F_COS[parc], F_COS[parc] + 1024), op=ALU.mult))
            r.emit(V, lambda e, parc=parc, sl=sl: e.tensor_tensor(
                out=FA(F_RT2, F_RT2 + 1024),
                in0=AFR[:, sl:sl + 512].bitcast(BF16),
                in1=FA(F_SINN[parc], F_SINN[parc] + 1024), op=ALU.mult))
            r.emit(V, lambda e, tc=tc, j=j: e.tensor_tensor(
                out=srcp(tc, j, 0, 1024), in0=FA(F_RT1, F_RT1 + 1024),
                in1=FA(F_RT2, F_RT2 + 1024), op=ALU.add), inc=[(vse, 1)])
            vse_rope[idx] = vse.n

    r.wait(G, vse, vse_wall)
    r.wait(TE, vse, vse_wall)
    r.wait(TE, sse, sse.n)
    for tc in range(8):
        for th in range(2):
            if th == 1 and tc >= 1:
                emit_vt(tc - 1)
            for hc2 in range(16):
                key = (tc * 2 + th) * 16 + hc2
                buf = key % 4
                if key >= 4:
                    r.wait(G, tse, tse_xuse[key - 4])
                r.emit(G, lambda e, tc=tc, th=th, hc2=hc2, buf=buf: e.dma_start(
                    out=A(R_XTS, buf * 1024, buf * 1024 + 1024).rearrange(
                        "p (g c) -> p g c", g=2),
                    in_=xt[hc2 * 256:(hc2 + 1) * 256,
                           tc * 1024 + th * 512:tc * 1024 + th * 512 + 512]
                    .rearrange("(g p) c -> p g c", p=128)), inc=[(gd, 16)])
                r.wait(TE, gd, gd.n)
                if hc2 == 0 and (tc * 2 + th) >= 1:
                    r.wait(TE, sse, sse_proj[tc * 2 + th - 1])
                for g in range(2):
                    hc = hc2 * 2 + g
                    for dt in range(6):
                        last = (dt == 5 and g == 1)
                        r.emit(TE, lambda e, dt=dt, hc=hc, buf=buf, g=g: e.matmul(
                            ptgt(dt),
                            wsl(dt, hc),
                            A(R_XTS, buf * 1024 + g * 512,
                              buf * 1024 + (g + 1) * 512),
                            start=(hc == 0), stop=(hc == 31)),
                            inc=[(tse, 1)] if last else None)
                tse_xuse[key] = tse.n
                if th == 0 and hc2 == 0 and tc >= 1:
                    emit_rope(tc - 1)
            r.wait(SC, tse, tse.n)
            for dt in range(6):
                r.emit(SC, lambda e, dt=dt, tc=tc, th=th: e.copy(
                    rawsl2(dt, tc, th * 512, th * 512 + 512), ptgt(dt)),
                    inc=[(sse, 1)] if dt == 5 else None)
            sse_proj[tc * 2 + th] = sse.n
    emit_vt(7)
    emit_rope(7)
    sse_projall = sse.n
    vse_ropeall = vse.n
    sse_vtall = sse.n

    pairs = [(b, t) for b in range(2) for t in range(32)]
    SCR = [PAp, PBp]
    scr_free = {}
    seg_ctr = [0]
    ptp = [PEp, PFp]
    ptp_free = {}
    pt_last = [0]
    sr_busy = {}
    yd_stage = [0]
    yd_stx = [0]
    cse_ax = [0]
    pd_free = [0]

    def lk(t):
        return (t + 1) * 128

    def prbsl(par, b, h, a, bb):
        if par == 0:
            return A(PRB[h], a, bb)
        base = R_QROPE[0] + h * T + b * S
        return ABF[:, base + a:base + bb]

    def diagp(par, h):
        return A(DIAG[h]) if par == 0 else A(DIAG2[h])

    pt_par = {0: 0, 1: 0}

    r.wait(TE, vse, vse_ropeall)
    r.wait(TE, sse, sse_vtall)

    def emit_head01(b, t):
        emit_head_full(b, t, 0)
        emit_head_full(b, t, 1)
        emit_cmb(t, 0)

    def emit_head_full(b, t, h):
        segs = (lk(t) + 1023) // 1024
        for s in range(segs):
            emit_seg_mm(b, t, h, s)
            emit_seg_evac(b, t, h, s)

    seg_reg = {}

    def emit_seg_mm(b, t, h, s):
        if True:
            a = s * 1024
            bnd = min(lk(t), a + 1024)
            w = bnd - a
            reg = seg_ctr[0] % 2
            seg_ctr[0] += 1
            seg_reg[(h, s)] = reg
            r.wait(TE, vse, scr_free.get(reg, 0))
            for half in range((w + 511) // 512):
                wa = half * 512
                wb = min(w, wa + 512)
                r.emit(TE, lambda e, b=b, t=t, h=h, a=a, wa=wa, wb=wb, reg=reg: e.matmul(
                    SCR[reg][:, wa:wb],
                    A(R_QROPE, h * T + b * S + t * 128, h * T + b * S + t * 128 + 128),
                    A(R_KROPE, b * S + a + wa, b * S + a + wb),
                    start=True, stop=True),
                    inc=[(tse, 1)] if wb == w else None)
            seg_tse[(h, s)] = tse.n

    def emit_seg_evac(b, t, h, s):
        srh = h % 2
        sro = SRA if srh == 0 else SRB
        if True:
            a = s * 1024
            bnd = min(lk(t), a + 1024)
            w = bnd - a
            reg = seg_reg[(h, s)]
            r.wait(V, tse, seg_tse[(h, s)])
            if s == 0:
                r.wait(V, sse, sr_busy.get(srh, 0))
            if a <= t * 128 < bnd:
                dloc = t * 128 - a
                r.emit(V, lambda e, reg=reg, dloc=dloc: e.tensor_tensor(
                    out=SCR[reg][:, dloc:dloc + 128], in0=SCR[reg][:, dloc:dloc + 128],
                    in1=tri_f[:], op=ALU.add))
            r.emit(V, lambda e, reg=reg, w=w, sro=sro, a=a, h=h, s=s: e.tensor_scalar(
                out=FA(sro + a, sro + a + w), in0=SCR[reg][:, 0:w],
                scalar1=SCALE, scalar2=FMIN,
                op0=ALU.mult, op1=ALU.max, accum_out=st(MXS[h][s])),
                inc=[(vse, 1)])
            scr_free[reg] = vse.n

    def emit_cmb(t, h):
        ns = (lk(t) + 1023) // 1024
        if ns == 2:
            r.emit(V, lambda e, h=h: e.tensor_tensor(
                out=st(MXC[h]), in0=st(MXS[h][0]), in1=st(MXS[h][1]), op=ALU.max))
        elif ns == 3:
            r.emit(V, lambda e, h=h: e.tensor_tensor(
                out=st(M01[h]), in0=st(MXS[h][0]), in1=st(MXS[h][1]), op=ALU.max))
            spacer()
            r.emit(V, lambda e, h=h: e.tensor_tensor(
                out=st(MXC[h]), in0=st(M01[h]), in1=st(MXS[h][2]), op=ALU.max))
        elif ns == 4:
            r.emit(V, lambda e, h=h: e.tensor_tensor(
                out=st(M01[h]), in0=st(MXS[h][0]), in1=st(MXS[h][1]), op=ALU.max))
            r.emit(V, lambda e, h=h: e.tensor_tensor(
                out=st(M23[h]), in0=st(MXS[h][2]), in1=st(MXS[h][3]), op=ALU.max))
            spacer()
            r.emit(V, lambda e, h=h: e.tensor_tensor(
                out=st(MXC[h]), in0=st(M01[h]), in1=st(M23[h]), op=ALU.max))

    def mxcol(t, h):
        ns = (lk(t) + 1023) // 1024
        return MXS[h][0] if ns == 1 else MXC[h]

    def emit_nmx(t, h):
        r.emit(V, lambda e, h=h, t=t: e.tensor_scalar(
            out=st(NMX[h]), in0=st(mxcol(t, h)), scalar1=-1.0, scalar2=None,
            op0=ALU.mult), inc=[(vse, 1)])

    def emit_exp(b, t, h, par):
        sro = SRA if h % 2 == 0 else SRB
        r.wait(SC, vse, vse.n)
        r.wait(SC, tse, pt_par[par])
        r.emit(SC, lambda e, h=h, t=t, sro=sro, b=b, par=par: e.activation(
            prbsl(par, b, h, 0, lk(t)), FA(sro, sro + lk(t)), AF.Exp,
            bias=st(NMX[h]), scale=1.0), inc=[(sse, 1)])
        sr_busy[h % 2] = sse.n
        exp_sse[h] = sse.n

    sms_junk = {}
    sms_direct = {}
    exp_sse = {}
    SMB = [40 + h for h in range(4)]

    def emit_cast(b, t, h, par):
        L = lk(t)
        L2 = (L // 256) * 128
        if L2 == 0:
            L2 = L
        r.emit(SC, lambda e, h=h, L2=L2, b=b, par=par: e.activation(
            A(R_JUNK, 0, L2),
            prbsl(par, b, h, 0, L2), AF.Copy, bias=0.0, scale=1.0,
            accum_out=st(SMS[h])), inc=[(sse, 1)])
        sms_junk[h] = sse.n
        sms_direct[h] = (L2 == L)
        if L2 < L:
            r.wait(V, sse, exp_sse[h])
            r.emit(V, lambda e, h=h, L2=L2, L=L, b=b, par=par: e.tensor_scalar(
                out=prbsl(par, b, h, L2, L), in0=prbsl(par, b, h, L2, L),
                scalar1=1.0, scalar2=0.0, op0=ALU.mult, op1=ALU.add,
                accum_out=st(SMB[h])), inc=[(vse, 1)])

    def emit_add(h):
        if sms_direct[h]:
            return
        r.wait(V, sse, sms_junk[h])
        r.emit(V, lambda e, h=h: e.tensor_tensor(
            out=st(SMS[h]), in0=st(SMS[h]), in1=st(SMB[h]), op=ALU.add))

    def emit_recip(h):
        if sms_direct[h]:
            r.wait(V, sse, sms_junk[h])
        r.emit(V, lambda e, h=h: e.reciprocal(st(RCP[h]), st(SMS[h])))

    def emit_diag(h, par, last=False):
        r.wait(V, tse, pt_par[par])
        r.emit(V, lambda e, h=h, par=par: e.tensor_scalar(
            out=diagp(par, h), in0=eye_bf[:], scalar1=st(RCP[h]), scalar2=None,
            op0=ALU.mult), inc=[(vse, 1)] if last else None)

    def build_part2_steps(b, t, nxt, par, vd_cell):
        steps = []
        steps.append(lambda: (emit_nmx(t, 0), emit_exp(b, t, 0, par),
                              emit_cmb(t, 1)))
        steps.append(lambda: emit_head_full(b, t, 2))
        steps.append(lambda: (emit_nmx(t, 1), emit_exp(b, t, 1, par),
                              emit_cmb(t, 2)))
        steps.append(lambda: emit_head_full(b, t, 3))
        steps.append(lambda: (emit_nmx(t, 2), emit_exp(b, t, 2, par),
                              emit_cmb(t, 3), spacer(), emit_nmx(t, 3),
                              emit_exp(b, t, 3, par)))
        steps.append(lambda: (emit_cast(b, t, 0, par), emit_cast(b, t, 1, par)))
        if nxt is not None:
            steps.append(lambda: emit_head_full(nxt[0], nxt[1], 0))
        steps.append(lambda: (emit_cast(b, t, 2, par), emit_cast(b, t, 3, par)))
        if nxt is not None:
            steps.append(lambda: emit_head_full(nxt[0], nxt[1], 1))

        def tail():
            for h in range(4):
                emit_add(h)
            for h in range(4):
                emit_recip(h)
            for h in range(4):
                emit_diag(h, par, last=(h == 3))
            if nxt is not None:
                emit_cmb(nxt[1], 0)
            vd_cell[0] = vse.n
        steps.append(tail)
        return steps

    seg_tse = {}

    def emit_pv(b, t, v_diag, par, steps=()):
        nkt = t + 1
        sidx = [0]

        def feed():
            if sidx[0] < len(steps):
                steps[sidx[0]]()
                sidx[0] += 1
        step_every = max(1, (nkt + 1) // max(1, len(steps)))
        r.wait(TE, vse, v_diag)
        if isinstance(pd_free[0], tuple):
            r.wait(TE, pd_free[0][0], pd_free[0][1])
        pt_ready = {}
        for kt in range(nkt + 1):
            if kt % step_every == 0:
                feed()
            if kt < nkt:
                bank = kt % 2
                sl = kt % 8
                if bank in ptp_free:
                    semx, vx = ptp_free[bank]
                    r.wait(TE, semx, vx)
                for h in range(4):
                    r.emit(TE, lambda e, h=h, kt=kt, bpar=par, b=b, bank=kt % 2: e.matmul(
                        ptp[bank][:, h * 128:(h + 1) * 128],
                        prbsl(bpar, b, h, kt * 128, kt * 128 + 128),
                        diagp(bpar, h), start=True, stop=True),
                        inc=[(tse, 1)] if h == 3 else None)
                t_pt = tse.n
                if kt % 3 != 2:
                    r.wait(SC, tse, t_pt)
                    r.emit(SC, lambda e, sl=sl, bank=bank: e.copy(
                        A(R_PTSEG, sl * 512, (sl + 1) * 512), ptp[bank][:]),
                        inc=[(sse, 1)])
                    ptp_free[bank] = (sse, sse.n)
                else:
                    r.wait(V, tse, t_pt)
                    r.emit(V, lambda e, sl=sl, bank=bank: e.tensor_copy(
                        A(R_PTSEG, sl * 512, (sl + 1) * 512), ptp[bank][:]),
                        inc=[(vse, 1)])
                    ptp_free[bank] = (vse, vse.n)
                pt_ready[kt] = ptp_free[bank]
            if kt >= 1:
                pk = kt - 1
                semx, vx = pt_ready[pk]
                r.wait(TE, semx, vx)
                r.emit(TE, lambda e, b=b, pk=pk, nkt=nkt: e.matmul(
                    PDp[:],
                    A(R_VSB, b * S + pk * 128, b * S + pk * 128 + 128),
                    A(R_PTSEG, (pk % 8) * 512, (pk % 8) * 512 + 512),
                    start=(pk == 0), stop=(pk == nkt - 1)),
                    inc=[(tse, 1)] if pk == nkt - 1 else None)
        pt_last[0] = tse.n
        pt_par[par] = tse.n
        while sidx[0] < len(steps):
            feed()
        r.wait(V, tse, tse.n)
        if t % 8 == 0:
            r.wait(V, yd, yd_stage[0])
        for h in range(4):
            r.emit(V, lambda e, h=h, t=t: e.tensor_copy(
                A(ASTG[h], (t % 8) * 128, (t % 8) * 128 + 128),
                PDp[:, h * 128:(h + 1) * 128]),
                inc=[(vse, 1)] if h == 3 else None)
        pd_free[0] = (vse, vse.n)
        if t % 8 == 3:
            shard = b * 4 + t // 8
            r.wait(SY, vse, vse.n)
            for h in range(4):
                r.emit(SY, lambda e, h=h, shard=shard: e.dma_start(
                    out=a2ix.ap()[shard, h * 128:(h + 1) * 128, :],
                    in_=A(ASTG[h], 0, 512)), inc=[(yd, 16)])
            yd_stage[0] = yd.n
            yd_stx[0] = yd.n
        if t % 8 == 7:
            shard = b * 4 + t // 8
            r.wait(SY, vse, vse.n)
            for h in range(4):
                r.emit(SY, lambda e, h=h, shard=shard: e.dma_start(
                    out=a2iy.ap()[shard, h * 128:(h + 1) * 128, :],
                    in_=A(ASTG[h], 512, 1024)), inc=[(yd, 16)])
            yd_stage[0] = yd.n

    prev = None
    emit_head01(pairs[0][0], pairs[0][1])
    for i, (b, t) in enumerate(pairs):
        if i == 61:
            r.wait(G, yd, yd_stx[0])
            r.emit(G, lambda e: e.collective_compute(
                "AllToAll", ALU.bypass, replica_groups=[list(range(NCORES))],
                ins=[a2ix.ap()], outs=[a2ox.ap()]), inc=[(cse, 1)])
            cse_ax[0] = cse.n
        nxt = pairs[i + 1] if i + 1 < len(pairs) else None
        vd_cell = [None]
        steps = build_part2_steps(b, t, nxt, i % 2, vd_cell)
        if prev is not None:
            emit_pv(prev[0], prev[1], prev[2], prev[3], steps)
        else:
            for s in steps:
                s()
        prev = (b, t, vd_cell[0], i % 2)
    emit_pv(prev[0], prev[1], prev[2], prev[3])
    r.wait(V, tse, tse.n)
    attn_v_end = vse.n
    attn_s_end = sse.n

    yd_a2 = yd.n
    r.wait(G, yd, yd_a2)
    r.emit(G, lambda e: e.collective_compute(
        "AllToAll", ALU.bypass, replica_groups=[list(range(NCORES))],
        ins=[a2iy.ap()], outs=[a2oy.ap()]), inc=[(cse, 1)])
    cse_a2 = cse.n

    r.wait(SY, cse, cse_ax[0])
    for hcg in range(32):
        rr, lr = hcg // 4, hcg % 4
        r.emit(SY, lambda e, hcg=hcg, rr=rr, lr=lr: e.dma_start(
            out=A(R_ATM, hcg * 1024, hcg * 1024 + 512),
            in_=a2ox.ap()[rr, lr * 128:(lr + 1) * 128, :]), inc=[(yd, 16)])
    yd_atm_x = yd.n
    r.wait(SY, cse, cse_a2)
    for hcg in range(32):
        rr, lr = hcg // 4, hcg % 4
        r.emit(SY, lambda e, hcg=hcg, rr=rr, lr=lr: e.dma_start(
            out=A(R_ATM, hcg * 1024 + 512, hcg * 1024 + 1024),
            in_=a2oy.ap()[rr, lr * 128:(lr + 1) * 128, :]), inc=[(yd, 16)])
    yd_atm = yd.n

    r.wait(G, cse, cse_a2)

    F_P1, F_P2, F_PJ = 2048, 2176, 2304
    OPS2 = [[PDp[:], PEp[:], PFp[:], PGp[:]],
            [PAp[:, 0:512], PAp[:, 512:1024], PBp[:, 0:512], PBp[:, 512:1024]]]
    NK = 16
    wo_use = {}
    v_wo = {}
    oq_guard = {}
    v_evac_p = {}
    xr_guard = {}
    first_mm = True
    for nk in range(NK):
        for gq in range(8):
            bo = (gq % 2) * 2048
            gk = nk * 8 + gq
            if gk >= 2:
                r.wait(G, vse, oq_guard[gk - 2])
            r.emit(G, lambda e, nk=nk, gq=gq, bo=bo: e.dma_start(
                out=A(R_OQS, bo, bo + 1024).rearrange("p (g c) -> p g c", g=4),
                in_=oc[gq * 512:(gq + 1) * 512, nk * 256:(nk + 1) * 256].rearrange(
                    "(g p) c -> p g c", p=128)), inc=[(gd, 16)])
            r.emit(G, lambda e, nk=nk, gq=gq, bo=bo: e.dma_start(
                out=A(R_OQS, bo + 1024, bo + 2048).rearrange("p (g c) -> p g c", g=4),
                in_=osr[gq * 512:(gq + 1) * 512, nk * 256:(nk + 1) * 256].rearrange(
                    "(g p) c -> p g c", p=128)), inc=[(gd, 16)])
            r.wait(V, gd, gd.n)
            if nk >= 1:
                r.wait(V, tse, wo_use[gq])
            r.emit(V, lambda e, gq=gq, bo=bo: e.scalar_tensor_tensor(
                out=A(R_WO, gq * 1024, (gq + 1) * 1024), in0=A(R_OQS, bo, bo + 1024),
                scalar=-8.0, in1=A(R_OQS, bo + 1024, bo + 2048),
                op0=ALU.add, op1=ALU.mult), inc=[(vse, 1)])
            for hcg in range(gq * 4, gq * 4 + 4):
                v_wo[(nk, hcg)] = vse.n
            oq_guard[gk] = vse.n
        for p in range(2):
            xb = (p % 2) * 1024
            grp = nk * 2 + p
            if grp < 2:
                r.wait(SY, vse, attn_v_end)
                r.wait(SY, sse, attn_s_end)
            r.wait(SY, vse, xr_guard.get(grp - 2, 0))
            for tt in range(4):
                gtt = p * 4 + tt
                r.emit(SY, lambda e, gtt=gtt, nk=nk, tt=tt, xb=xb: e.dma_start(
                    out=FA(XRS + xb + tt * 256, XRS + xb + (tt + 1) * 256),
                    in_=x_own[gtt * 128:(gtt + 1) * 128, nk * 256:(nk + 1) * 256]),
                    inc=[(yd, 16)])
            yd_x = yd.n
            if first_mm:
                r.wait(TE, yd, yd_atm_x)
                first_mm = False
            if grp == 1:
                r.wait(TE, yd, yd_atm)
            par = grp % 2
            r.wait(TE, vse, v_evac_p.get(grp - 2, 0))
            for hcg in range(32):
                r.wait(TE, vse, v_wo[(nk, hcg)])
                for tt in range(4):
                    r.emit(TE, lambda e, hcg=hcg, p=p, tt=tt, par=par: e.matmul(
                        OPS2[par][tt][:, 0:256],
                        A(R_ATM, hcg * 1024 + p * 512 + tt * 128,
                          hcg * 1024 + p * 512 + tt * 128 + 128),
                        A(R_WO, hcg * 256, (hcg + 1) * 256),
                        start=(hcg == 0), stop=(hcg == 31)),
                        inc=[(tse, 1)] if tt == 3 else None)
                wo_use[hcg // 4] = tse.n
            r.wait(V, tse, tse.n)
            r.wait(V, yd, yd_x)
            for tt in range(4):
                gtt = p * 4 + tt
                r.emit(V, lambda e, gtt=gtt, nk=nk, tt=tt, xb=xb, par=par: e.scalar_tensor_tensor(
                    out=A(R_HPRE, gtt * 4096 + nk * 256, gtt * 4096 + (nk + 1) * 256),
                    in0=OPS2[par][tt][:, 0:256], scalar=0.25,
                    in1=FA(XRS + xb + tt * 256, XRS + xb + (tt + 1) * 256),
                    op0=ALU.mult, op1=ALU.add,
                    accum_out=FA(F_P1 + gtt * 16 + nk, F_P1 + gtt * 16 + nk + 1)),
                    inc=[(vse, 1)] if tt == 3 else None)
            for tt in range(4):
                gtt = p * 4 + tt
                r.emit(V, lambda e, gtt=gtt, nk=nk: e.scalar_tensor_tensor(
                    out=FA(F_PJ, F_PJ + 256),
                    in0=A(R_HPRE, gtt * 4096 + nk * 256, gtt * 4096 + (nk + 1) * 256),
                    scalar=1.0,
                    in1=A(R_HPRE, gtt * 4096 + nk * 256, gtt * 4096 + (nk + 1) * 256),
                    op0=ALU.mult, op1=ALU.mult,
                    accum_out=FA(F_P2 + gtt * 16 + nk, F_P2 + gtt * 16 + nk + 1)))
            v_evac_p[grp] = vse.n
            xr_guard[grp] = vse.n
    v_eall = vse.n


    def emit_rsqrt_cols(vpe_c, rstd_c, nt1_c, nt2_c, nt3_c, w=1):
        vpe, rstd = st(vpe_c, w), st(rstd_c, w)
        n1, n2, n3 = st(nt1_c, w), st(nt2_c, w), st(nt3_c, w)
        r.emit(V, lambda e: e.tensor_scalar(
            out=n1.bitcast(I32), in0=vpe.bitcast(I32), scalar1=1,
            scalar2=None, op0=ALU.arith_shift_right))
        spacer()
        r.emit(V, lambda e: e.tensor_scalar(
            out=rstd.bitcast(I32), in0=n1.bitcast(I32), scalar1=-1,
            scalar2=0x5F3759DF, op0=ALU.mult, op1=ALU.add))
        spacer()
        for _ in range(3):
            r.emit(V, lambda e: e.tensor_tensor(out=n1, in0=rstd, in1=rstd, op=ALU.mult))
            spacer()
            r.emit(V, lambda e: e.tensor_tensor(out=n2, in0=vpe, in1=n1, op=ALU.mult))
            spacer()
            r.emit(V, lambda e: e.tensor_scalar(
                out=n3, in0=n2, scalar1=-0.5, scalar2=1.5, op0=ALU.mult, op1=ALU.add))
            spacer()
            r.emit(V, lambda e: e.tensor_tensor(out=rstd, in0=rstd, in1=n3, op=ALU.mult))
            spacer()

    yd_out = [0]
    r.wait(V, vse, v_eall)

    def hp(a, b, tt):
        return A(R_HPRE, tt * 4096 + a, tt * 4096 + b)

    def hh(a, b, tt):
        return A(R_H, tt * 4096 + a, tt * 4096 + b)

    for tt in range(8):
        r.emit(V, lambda e, tt=tt: e.reduce_sum(
            st(B_S1 + tt), FA(F_P1 + tt * 16, F_P1 + tt * 16 + 16),
            axis=mybir.AxisListType.X))
        r.emit(V, lambda e, tt=tt: e.reduce_sum(
            st(B_S2 + tt), FA(F_P2 + tt * 16, F_P2 + tt * 16 + 16),
            axis=mybir.AxisListType.X))
    spacer(2)
    INV_N = 1.0 / 4096.0
    r.emit(V, lambda e: e.tensor_scalar(
        out=st(B_MEAN, 8), in0=st(B_S1, 8), scalar1=INV_N, scalar2=None, op0=ALU.mult))
    r.emit(V, lambda e: e.tensor_scalar(
        out=st(B_EVAR, 8), in0=st(B_S2, 8), scalar1=INV_N, scalar2=float(EPS),
        op0=ALU.mult, op1=ALU.add))
    r.emit(V, lambda e: e.tensor_scalar(
        out=st(B_NMEAN, 8), in0=st(B_MEAN, 8), scalar1=-1.0, scalar2=None, op0=ALU.mult))
    r.emit(V, lambda e: e.tensor_tensor(
        out=st(B_MSQ, 8), in0=st(B_MEAN, 8), in1=st(B_MEAN, 8), op=ALU.mult))
    spacer()
    r.emit(V, lambda e: e.tensor_tensor(
        out=st(B_VPE, 8), in0=st(B_EVAR, 8), in1=st(B_MSQ, 8), op=ALU.subtract))
    spacer()
    emit_rsqrt_cols(B_VPE, B_RSTD, B_S1, B_S2, B_MSQ, w=8)
    r.emit(V, lambda e: e.tensor_tensor(
        out=st(B_NB, 8), in0=st(B_NMEAN, 8), in1=st(B_RSTD, 8), op=ALU.mult))
    spacer(2)
    r.emit(V, lambda e: e.nop(), inc=[(vse, 1)])

    LG = [PFp, PGp]
    v_lgread = {}
    t_lg_map = {}
    s_h_map = {}
    exp_sse_f = {}

    def emit_router(tt):
        lgb = LG[tt % 2]
        r.emit(V, lambda e, tt=tt: e.tensor_scalar(
            out=hh(0, 4096, tt), in0=hp(0, 4096, tt), scalar1=st(B_RSTD + tt),
            scalar2=st(B_NB + tt), op0=ALU.mult, op1=ALU.add), inc=[(vse, 1)])
        v_h = vse.n
        r.wait(SC, vse, v_h)
        r.emit(SC, lambda e: e.nop(), inc=[(sse, 1)])
        s_h_map[tt] = sse.n
        if tt >= 2:
            r.wait(TE, vse, v_lgread[tt - 2])
        ht_ready = {}
        for hc in range(34):
            if hc >= 2:
                mm = hc - 2
                slot = mm % 2
                r.wait(TE, sse, ht_ready[mm])
                r.emit(TE, lambda e, mm=mm, slot=slot, lgb=lgb: e.matmul(
                    lgb[:, 0:E], A(R_HT, slot * 128, (slot + 1) * 128),
                    wrt_sb[:, mm * E:(mm + 1) * E],
                    start=(mm == 0), stop=False), inc=[(tse, 1)])
            if hc < 32:
                slot = hc % 2
                r.wait(TE, sse, s_h_map[tt])
                if hc >= 2:
                    r.wait(TE, sse, ht_ready[hc - 2])
                pp = PEp if slot == 0 else PDp
                r.emit(TE, lambda e, hc=hc, pp=pp, tt=tt: e.transpose(
                    pp[:, 0:64].bitcast(BF16),
                    hh(hc * 128, (hc + 1) * 128, tt), eye_bf[:]), inc=[(tse, 1)])
                t_tr = tse.n
                r.wait(SC, tse, t_tr)
                r.emit(SC, lambda e, slot=slot, pp=pp: e.copy(
                    A(R_HT, slot * 128, (slot + 1) * 128),
                    pp[:, 0:64].bitcast(BF16)), inc=[(sse, 1)])
                ht_ready[hc] = sse.n
        r.emit(TE, lambda e, lgb=lgb: e.matmul(
            lgb[:, 0:E], one_bf[:], rb_bf[:], start=False, stop=True), inc=[(tse, 1)])
        t_lg_map[tt] = tse.n

    def emit_soft_head(tt):
        lgb = LG[tt % 2]
        r.wait(V, tse, t_lg_map[tt])
        if tt >= 1:
            r.wait(V, sse, exp_sse_f[tt - 1])
        r.emit(V, lambda e, lgb=lgb: e.tensor_scalar(
            out=lg_sb[:], in0=lgb[:, 0:E], scalar1=1.0, scalar2=FMIN,
            op0=ALU.mult, op1=ALU.max, accum_out=st(C_M64)), inc=[(vse, 1)])
        v_lgread[tt] = vse.n
        spacer()
        r.emit(V, lambda e: e.tensor_scalar(
            out=st(C_NM64), in0=st(C_M64), scalar1=-1.0, scalar2=None,
            op0=ALU.mult), inc=[(vse, 1)])
        r.wait(SC, vse, vse.n)
        r.emit(SC, lambda e: e.activation(
            ex_sb[:], lg_sb[:], AF.Exp, bias=st(C_NM64), scale=1.0), inc=[(sse, 1)])
        exp_sse_f[tt] = sse.n

    def emit_soft_tail(tt):
        r.wait(V, sse, exp_sse_f[tt])
        r.emit(V, lambda e: e.tensor_scalar(
            out=cur_sb[:, 0:E], in0=ex_sb[:], scalar1=1.0, scalar2=0.0,
            op0=ALU.mult, op1=ALU.add, accum_out=st(C_SUM0)))
        r.emit(V, lambda e: e.memset(st(C_WS), 0.0))
        for j in range(8):
            ca = (j % 2) * E
            cb = ((j + 1) % 2) * E
            r.emit(V, lambda e, ca=ca: e.reduce_max(
                st(C_MJ), cur_sb[:, ca:ca + E], axis=mybir.AxisListType.X))
            spacer(2)
            r.emit(V, lambda e: e.tensor_scalar(
                out=st(C_MJC), in0=st(C_MJ), scalar1=0.0, scalar2=None,
                op0=ALU.max))
            if j < 7:
                r.emit(V, lambda e, ca=ca: e.tensor_scalar(
                    out=msk_sb[:], in0=cur_sb[:, ca:ca + E], scalar1=st(C_MJ),
                    scalar2=None, op0=ALU.is_ge))
                r.emit(V, lambda e: e.tensor_tensor(
                    out=st(C_WS), in0=st(C_WS), in1=st(C_MJC), op=ALU.add))
                r.emit(V, lambda e, ca=ca, cb=cb: e.scalar_tensor_tensor(
                    out=cur_sb[:, cb:cb + E], in0=msk_sb[:], scalar=-1e30,
                    in1=cur_sb[:, ca:ca + E], op0=ALU.mult, op1=ALU.add))
                spacer()
            else:
                spacer()
                r.emit(V, lambda e: e.tensor_tensor(
                    out=st(C_WS), in0=st(C_WS), in1=st(C_MJC), op=ALU.add))
        r.emit(V, lambda e: e.tensor_scalar(
            out=msk_sb[:], in0=ex_sb[:], scalar1=st(C_MJ), scalar2=None,
            op0=ALU.is_ge))
        r.emit(V, lambda e: e.tensor_tensor(
            out=st(C_COEF), in0=st(C_SUM0), in1=st(C_WS), op=ALU.add))
        r.emit(V, lambda e: e.tensor_tensor(
            out=A(R_WDS), in0=ex_sb[:], in1=msk_sb[:], op=ALU.mult),
            inc=[(vse, 1)])
        v_wd = vse.n
        r.wait(TE, vse, v_wd)
        r.emit(TE, lambda e: e.transpose(
            PAp[0:E, 0:64].bitcast(BF16), A(R_WDS), eye_bf[:]), inc=[(tse, 1)])
        r.wait(SC, tse, tse.n)
        r.emit(SC, lambda e: e.copy(
            ABF[0:E, 75072:75072 + 128], PAp[0:E, 0:64].bitcast(BF16)),
            inc=[(sse, 1)])
        s_wdt = sse.n
        r.wait(TE, sse, s_wdt)
        v_gun = {}
        for nb4 in range(4):
            ps = PAp if nb4 % 2 == 0 else PBp
            if nb4 >= 2:
                r.wait(TE, vse, v_gun[nb4 - 2])
            for half in range(2):
                r.emit(TE, lambda e, ps=ps, nb4=nb4, half=half: e.matmul(
                    ps[:, half * 512:(half + 1) * 512], ABF[0:E, 75072:75072 + 128],
                    eb_sb[:, nb4 * 1024 + half * 512:nb4 * 1024 + (half + 1) * 512],
                    start=True, stop=True), inc=[(tse, 1)] if half == 1 else None)
            r.wait(V, tse, tse.n)
            r.emit(V, lambda e, ps=ps, nb4=nb4, tt=tt: e.scalar_tensor_tensor(
                out=FA(GUN + nb4 * 1024, GUN + (nb4 + 1) * 1024),
                in0=hh(nb4 * 1024, (nb4 + 1) * 1024, tt), scalar=st(C_COEF),
                in1=ps[:], op0=ALU.mult, op1=ALU.add, accum_out=st(C_P1 + nb4)))
            r.emit(V, lambda e, nb4=nb4: e.scalar_tensor_tensor(
                out=FA(OUTB, OUTB + 1024),
                in0=FA(GUN + nb4 * 1024, GUN + (nb4 + 1) * 1024), scalar=1.0,
                in1=FA(GUN + nb4 * 1024, GUN + (nb4 + 1) * 1024),
                op0=ALU.mult, op1=ALU.mult, accum_out=st(C_P2 + nb4)),
                inc=[(vse, 1)])
            v_gun[nb4] = vse.n
        r.emit(V, lambda e: e.reduce_sum(
            st(C_SUM), st(C_P1, 4), axis=mybir.AxisListType.X))
        r.emit(V, lambda e: e.reduce_sum(
            st(C_SSQ), st(C_P2, 4), axis=mybir.AxisListType.X))
        spacer(2)
        r.emit(V, lambda e: e.tensor_scalar(
            out=st(C_MEAN), in0=st(C_SUM), scalar1=INV_N, scalar2=None, op0=ALU.mult))
        r.emit(V, lambda e: e.tensor_scalar(
            out=st(C_EVAR), in0=st(C_SSQ), scalar1=INV_N, scalar2=float(EPS),
            op0=ALU.mult, op1=ALU.add))
        r.emit(V, lambda e: e.tensor_tensor(
            out=st(C_MSQ), in0=st(C_MEAN), in1=st(C_MEAN), op=ALU.mult))
        spacer()
        r.emit(V, lambda e: e.tensor_tensor(
            out=st(C_VPE), in0=st(C_EVAR), in1=st(C_MSQ), op=ALU.subtract))
        spacer()
        emit_rsqrt_cols(C_VPE, C_RSTD, C_NT1, C_NT2, C_NT3)
        r.emit(V, lambda e: e.tensor_scalar(
            out=st(C_NB), in0=st(C_MEAN), scalar1=st(C_RSTD), scalar2=-1.0,
            op0=ALU.mult, op1=ALU.mult))
        spacer()
        r.wait(V, yd, yd_out[0])
        r.emit(V, lambda e: e.tensor_scalar(
            out=FA(OUTB, OUTB + 4096), in0=FA(GUN, GUN + 4096),
            scalar1=st(C_RSTD), scalar2=st(C_NB), op0=ALU.mult, op1=ALU.add),
            inc=[(vse, 1)])
        r.wait(SY, vse, vse.n)
        r.emit(SY, lambda e, tt=tt: e.dma_start(
            out=out[tt * 128:(tt + 1) * 128, :], in_=FA(OUTB, OUTB + 4096)),
            inc=[(yd, 16)])
        yd_out[0] = yd.n

    emit_router(0)
    for tt in range(8):
        emit_soft_head(tt)
        if tt + 1 < 8:
            emit_router(tt + 1)
        emit_soft_tail(tt)

    r.wait(SY, yd, yd.n)

    with nc.Block() as blk:
        @blk.tensor
        def _(e):
            r.play("tensor", e)

        @blk.vector
        def _(e):
            r.play("vector", e)

        @blk.scalar
        def _(e):
            r.play("scalar", e)

        @blk.gpsimd
        def _(e):
            r.play("gpsimd", e)

        @blk.sync
        def _(e):
            r.play("sync", e)

    ctx.close()
    return nc


_NC_CACHE = [None]


def kernel(L0_attn_x, L0_attn_q_codes, L0_attn_k_codes, L0_attn_v_codes,
           L0_attn_o_codes, L0_attn_q_scales, L0_attn_k_scales,
           L0_attn_v_scales, L0_attn_o_scales, L0_ffn_router_weight,
           L0_ffn_router_bias, L0_ffn_expert_bias):
    xf = np.ascontiguousarray(np.asarray(L0_attn_x, np.float32).reshape(T, DM))
    qcod = np.ascontiguousarray(np.asarray(L0_attn_q_codes, np.int32))
    kcod = np.ascontiguousarray(np.asarray(L0_attn_k_codes, np.int32))
    vcod = np.ascontiguousarray(np.asarray(L0_attn_v_codes, np.int32))
    ocod = np.ascontiguousarray(np.asarray(L0_attn_o_codes, np.int32))

    def rep(s):
        return np.ascontiguousarray(np.repeat(np.asarray(s, np.float32), 128, axis=0))

    qsr, ksr, vsr, osrr = (rep(s) for s in (
        L0_attn_q_scales, L0_attn_k_scales, L0_attn_v_scales, L0_attn_o_scales))
    rww = np.ascontiguousarray(np.asarray(L0_ffn_router_weight, np.float32))
    rbb = np.ascontiguousarray(np.asarray(L0_ffn_router_bias, np.float32).reshape(1, E))
    ebb = np.ascontiguousarray(np.asarray(L0_ffn_expert_bias, np.float32))

    half = 64
    inv = (1.0 / (10000.0 ** (np.arange(half, dtype=np.float32) / np.float32(half)))).astype(np.float32)
    pos = np.arange(S, dtype=np.float32)
    ang = inv[:, None] * pos[None, :]
    ang2 = np.concatenate([ang, ang], axis=0).astype(np.float32)
    cosT = np.cos(ang2).astype(np.float32)
    sinT = np.sin(ang2).astype(np.float32)
    sinN = np.concatenate([-sinT[0:64], sinT[64:128]], 0)
    eye = np.eye(128, dtype=np.float32)
    rotMT = np.zeros((128, 128), np.float32)
    for d in range(64):
        rotMT[d + 64, d] = -1.0
        rotMT[d, d + 64] = 1.0
    tri = np.triu(np.full((128, 128), np.float32(-1.0e30), np.float32), k=1)
    ones1 = np.ones((1, 128), np.float32)

    xtf = np.ascontiguousarray(xf.T)
    in_maps = []
    for i in range(NCORES):
        in_maps.append(dict(
            x_own=np.ascontiguousarray(xf[i * TPC:(i + 1) * TPC]),
            xt=xtf,
            qc=np.ascontiguousarray(qcod[:, i * QSL:(i + 1) * QSL]),
            kc=np.ascontiguousarray(kcod[:, i * D:(i + 1) * D]),
            vc=np.ascontiguousarray(vcod[:, i * D:(i + 1) * D]),
            oc=ocod,
            qs=np.ascontiguousarray(qsr[:, i * QSL:(i + 1) * QSL]),
            ks=np.ascontiguousarray(ksr[:, i * D:(i + 1) * D]),
            vs=np.ascontiguousarray(vsr[:, i * D:(i + 1) * D]),
            osr=osrr,
            rw=rww, rb=rbb, ebi=ebb,
            cosc=cosT, sinc=sinN, eyec=eye, tric=tri, onec=ones1,
        ))

    if _NC_CACHE[0] is None:
        with allow_ici_all_to_all():
            _NC_CACHE[0] = build_nc()
    res = run_bass_kernel_spmd(_NC_CACHE[0], in_maps, core_ids=list(range(NCORES)))
    full = np.concatenate([res.results[i]["out"] for i in range(NCORES)], axis=0)
    return full.reshape(B, S, DM).astype(np.float32)
